# version 17
# baseline (speedup 1.0000x reference)
import numpy as np
import ml_dtypes
from contextlib import ExitStack

import concourse.bass as bass
import concourse.mybir as mybir
from concourse.bass_utils import run_bass_kernel_spmd

F32 = mybir.dt.float32
BF16 = mybir.dt.bfloat16
AF = mybir.ActivationFunctionType
AL = mybir.AluOpType
AP = bass.AP

NCORES = 8
B_PER = 4
S = 2048
D = 1024
NDC = D // 128
H = 16
HD = 64
SJ = 512
NJ = S // SJ
NKT = S // 128
POS = S // NCORES
T = 32

LN_EPS = 1e-5
EXP_BIAS = -4.0

ENGINES = ("sync", "gpsimd", "tensor", "vector", "scalar")


class Plan:
    def __init__(self, nc, ctx):
        self.nc = nc
        self.ctx = ctx
        self.ops = {e: [] for e in ENGINES}
        self.sems = {}
        self.counts = {}
        self.pending = {e: [] for e in ENGINES}
        self.s7 = None
        for name in ("pe", "act", "dve", "pl", "cc"):
            self.get_sem(name)

    def barrier(self, exclude=()):
        snap = [(n, c) for n, c in self.counts.items()
                if c and n not in exclude]
        for e in ENGINES:
            self.pending[e] = list(snap)

    def get_sem(self, name):
        if name not in self.sems:
            self.sems[name] = self.ctx.enter_context(self.nc.semaphore(name))
            self.counts[name] = 0
        return self.sems[name]

    def emit(self, eng, fn, waits=(), inc=None, inc_n=1):
        waits = tuple(self.pending[eng] + [w for w in waits if w is not None])
        if self.pending[eng]:
            self.pending[eng] = []
        if inc is not None:
            self.get_sem(inc)
        incs = []
        if inc is not None:
            self.counts[inc] += inc_n
            incs.append((inc, self.counts[inc], inc_n))
        self.ops[eng].append((waits, fn, tuple(incs)))
        return (inc, self.counts[inc]) if inc else None

    def dma(self, eng, out, in_, sem, waits=(), **kw):
        self.get_sem(sem)
        return self.emit(eng, lambda e: e.dma_start(out=out, in_=in_, **kw),
                         waits, inc=sem, inc_n=16)

    def mm(self, out, lhsT, rhs, start, stop, waits=(), tp=None, inc=False,
           sgc=False):
        kw = {}
        if tp is not None:
            kw["tile_position"] = tp
        if sgc:
            kw["skip_group_check"] = True
        return self.emit(
            "tensor",
            lambda e: e.matmul(out, lhsT, rhs, start=start, stop=stop, **kw),
            waits, inc=("pe" if inc else None))

    def act(self, out, in_, func, waits=(), bias=0.0, scale=1.0, accum=None):
        kw = {"bias": bias, "scale": scale}
        if accum is not None:
            kw["accum_out"] = accum
        return self.emit("scalar",
                         lambda e: e.activation(out, in_, func, **kw),
                         waits, inc="act")

    def dve(self, fname, *args, waits=(), **kw):
        return self.emit("vector", lambda e: getattr(e, fname)(*args, **kw),
                         waits, inc="dve")

    def pool(self, fname, *args, waits=(), **kw):
        waits = list(waits) + [("pl", self.counts["pl"])]
        return self.emit("gpsimd", lambda e: getattr(e, fname)(*args, **kw),
                         waits, inc="pl", inc_n=1)

    def replay(self, block):
        own = {"vector": "dve", "scalar": "act"}
        def runner(engname):
            def f(eng):
                last = {}
                i = 0
                osem = own.get(engname)
                for waits, fn, incs in self.ops[engname]:
                    if osem is not None and i > 0:
                        ws = list(waits) + [(osem, i)]
                    else:
                        ws = waits
                    for s, v in ws:
                        if last.get(s, 0) < v:
                            eng.wait_ge(self.sems[s], v)
                            last[s] = v
                    inst = fn(eng)
                    for s, v, n in incs:
                        inst.then_inc(self.sems[s], n)
                    i += 1
            return f
        block.sync(runner("sync"))
        block.gpsimd(runner("gpsimd"))
        block.tensor(runner("tensor"))
        block.vector(runner("vector"))
        block.scalar(runner("scalar"))


_SBT_N = [0]


def sbt(nc, ctx, name, shape, dt):
    _SBT_N[0] += 1
    return ctx.enter_context(nc.sbuf_tensor("%s_%d" % (name, _SBT_N[0]),
                                            shape, dt))


def pst(nc, ctx, name, shape, dt=F32):
    return ctx.enter_context(nc.psum_tensor(name, shape, dt))


def make_ln(nc, ctx, p, ntok, ps_stat, ps_bc, pfx="", use_s7=True,
            dma_q="gpsimd"):
    xbf = sbt(nc, ctx, "ln_xbf", [128, 2 * ntok], BF16)
    xsq = sbt(nc, ctx, "ln_xsq", [128, ntok], BF16)
    rows = sbt(nc, ctx, "ln_rows", [128, 2 * ntok], F32)
    rows_bf = sbt(nc, ctx, "ln_rows_bf", [1, 2 * ntok], BF16)
    mrbc = sbt(nc, ctx, "ln_mrbc", [128, ntok], BF16)
    ones_bf = sbt(nc, ctx, "ln_ones_bf", [128, 1], BF16)
    ones1f = sbt(nc, ctx, "ln_ones1f", [1, 128], BF16)
    eps_sb = sbt(nc, ctx, "ln_eps", [1, 1], F32)
    p.pool("memset", ones_bf[:], 1.0)
    p.pool("memset", ones1f[:], 1.0)
    p.pool("memset", eps_sb[:], LN_EPS)

    def ln(xrows, y_view, wait_loads=(), prev=None, y_done=None, y_war=None):
        prev = prev or {}
        t_x = []
        stat_last = None
        ld_prev = [prev.get("ld0"), prev.get("ld1")]
        sq_prev = prev.get("sq")
        rd_prev = [None, None]
        mm_prev = [None, None]
        for dc in range(NDC):
            o = (dc % 2) * ntok
            tb = p.dma(dma_q, xbf[:, o:o + ntok], xrows(dc),
                       sem=pfx + "lnx%d" % (dc % 2),
                       waits=list(wait_loads) + [ld_prev[dc % 2],
                                                 rd_prev[dc % 2],
                                                 mm_prev[dc % 2],
                                                 prev.get("apply")])
            ld_prev[dc % 2] = tb
            t_x.append(tb)
            p.mm(ps_stat[0:1, 0:ntok], ones_bf[:, 0:1],
                 xbf[:, o:o + ntok],
                 start=(dc == 0), stop=(dc == NDC - 1),
                 waits=[tb] + ([prev.get("rows_dve"),
                                p.s7 if use_s7 else prev.get("apply")]
                               if dc == 0 else []),
                 sgc=True)
            ts_ = p.act(xsq[:], xbf[:, o:o + ntok], AF.Square,
                        waits=[tb, sq_prev])
            rd_prev[dc % 2] = ts_
            stat_last = p.mm(ps_stat[32:33, 0:ntok], ones_bf[:, 0:1], xsq[:],
                             start=(dc == 0), stop=(dc == NDC - 1),
                             waits=[ts_], inc=True, sgc=True)
            mm_prev[dc % 2] = stat_last
            sq_prev = stat_last
        m_row = rows[64:65, 0:ntok]
        m2_row = rows[64:65, ntok:2 * ntok]
        var_row = rows[0:1, ntok:2 * ntok]
        r_row = rows[0:1, 0:ntok]
        p.dve("tensor_scalar_mul", m_row, ps_stat[0:1, 0:ntok], 1.0 / D,
              waits=[stat_last])
        p.dve("tensor_mul", m2_row, m_row, m_row)
        tv = p.dve("scalar_tensor_tensor", var_row, ps_stat[32:33, 0:ntok],
                   1.0 / D, m2_row, AL.mult, AL.subtract)
        tstd = p.act(var_row, var_row, AF.Sqrt, bias=eps_sb[:], waits=[tv])
        tr0 = p.dve("reciprocal", r_row, var_row, waits=[tstd])
        rbf = rows_bf[:, 0:ntok]
        mrbf = rows_bf[:, ntok:2 * ntok]
        tr = p.dve("tensor_copy", rbf, r_row)
        tmr = p.dve("scalar_tensor_tensor", mrbf, ps_stat[0:1, 0:ntok],
                    1.0 / D, r_row, AL.mult, AL.mult)
        tmb = p.mm(ps_bc[:, 0:ntok], ones1f[0:1, :], mrbf, start=True,
                   stop=True, waits=[tmr, tv, prev.get("mrbc_dve")], inc=True)
        tme = p.dve("tensor_copy", mrbc[:], ps_bc[:, 0:ntok], waits=[tmb])
        trb = p.mm(ps_bc[:, 0:ntok], ones1f[0:1, :], rbf, start=True,
                   stop=True, waits=[tr, tme, prev.get("apply")], inc=True)
        ty = None
        mul_prev = [None, None]
        for dc in range(NDC):
            o = (dc % 2) * ntok
            tb2 = p.dma(dma_q, xbf[:, o:o + ntok], xrows(dc),
                        sem=pfx + "lnx%d" % (dc % 2),
                        waits=[ld_prev[dc % 2], mul_prev[dc % 2],
                               stat_last if dc < 2 else None])
            ld_prev[dc % 2] = tb2
            tm = p.dve("tensor_mul", y_view(dc), xbf[:, o:o + ntok],
                       ps_bc[:, 0:ntok],
                       waits=[tb2, trb] + (y_war(dc) if y_war else []))
            mul_prev[dc % 2] = tm
            ty = p.dve("tensor_sub", y_view(dc), y_view(dc), mrbc[:])
            if y_done is not None:
                y_done(dc, ty)
        if use_s7:
            p.s7 = ty
        return {"apply": ty, "stats": stat_last, "rows_dve": tv,
                "mrbc_dve": tme, "x_loads": t_x, "ld0": ld_prev[0],
                "ld1": ld_prev[1], "sq": sq_prev}
    return ln


def emit_attn(nc, p, psX, dr, rope, nb=B_PER, pfx="", post=None):
    import collections
    x_d = dr["x"]
    x1_d = dr["x_out"]
    if rope:
        wqkv_d = dr["wqkvT"]
    else:
        wq_d, wk_d, wv_d = dr["Wq"], dr["Wk"], dr["Wv"]
    wout_d = dr["woutT"]
    bout_d = dr["bout_rs"]
    if rope:
        cos_d, sin_d = dr["cos_rep"], dr["sin_rep"]

    with ExitStack() as ctx:
        VST = 1536
        wbuf = sbt(nc, ctx, "wbuf", [128, 8 * D], BF16)
        y_sb = sbt(nc, ctx, "y_sb", [128, NDC * S], BF16)
        stage = sbt(nc, ctx, "stage", [128, 4096], BF16)
        q_sb = sbt(nc, ctx, "q_sb", [128, NDC * S], BF16)
        k_sb = sbt(nc, ctx, "k_sb", [128, NDC * S], BF16)
        v_sb = sbt(nc, ctx, "v_sb", [128, NKT * VST], BF16)
        av_sb = sbt(nc, ctx, "av_sb", [128, NDC * SJ], BF16)
        rs0 = sbt(nc, ctx, "rs0", [128, 512], BF16)
        rs1 = sbt(nc, ctx, "rs1", [128, 512], BF16)
        if rope:
            cos_sb = sbt(nc, ctx, "cos_sb", [128, S], BF16)
            sin_sb = sbt(nc, ctx, "sin_sb", [128, S], BF16)
        recip_sb = sbt(nc, ctx, "recip_sb", [1, SJ], BF16)
        rb_sb = sbt(nc, ctx, "rb_sb", [128, SJ], F32)
        rb_bf = sbt(nc, ctx, "rb_bf", [128, SJ], BF16)
        ones64f = sbt(nc, ctx, "ones64f", [1, 64], BF16)
        bout_sb = sbt(nc, ctx, "bout_sb", [128, NDC], F32)
        x1t = sbt(nc, ctx, "x1t", [128, SJ], F32)
        wo_ec = [sbt(nc, ctx, f"wo_ec{i}", [128, 1024], BF16) for i in range(2)]
        ebias_sb = sbt(nc, ctx, "ebias", [128, 1], F32)

        BG7 = psX[3][:, 512:1024]
        ln = make_ln(nc, ctx, p, SJ, ps_stat=psX[3][0:64, 512:1024],
                     ps_bc=BG7, pfx=pfx)
        if post is not None:
            y1buf = sbt(nc, ctx, "y1buf", [128, SJ], BF16)
            y1war = {}

            def y1_thunk(b, j, wl):
                def th(b=b, j=j, wl=wl):
                    def xrows(dc, b=b, j=j):
                        return post["x1_ap"][b, dc * 128:(dc + 1) * 128,
                                             j * SJ:(j + 1) * SJ]
                    def yv(dc):
                        return y1buf[:, :]
                    def ywar(dc):
                        return [y1war.get(0), y1war.get(1)]
                    def ydone(dc, ty, b=b, j=j):
                        for r in (0, 1):
                            tok = p.dma(
                                "gpsimd", post["stage"](b, j, dc, r),
                                y1buf[:, r * POS:(r + 1) * POS],
                                sem=pfx + "y1s%d" % r, waits=[ty])
                            y1war[r] = tok
                    st["ln_prev"] = ln(xrows, yv, wait_loads=wl,
                                       prev=st.get("ln_prev"),
                                       y_done=ydone, y_war=ywar)
                return th

            def y1_coll(k):
                w = [(pfx + "y1s%d" % r, p.counts.get(pfx + "y1s%d" % r, 0))
                     for r in (0, 1)]
                post["collective"](k, [t for t in w if t[1]])

        p.pool("memset", ones64f[:], 1.0)
        p.pool("memset", ebias_sb[:], EXP_BIAS)
        t_ones = None
        for pr in range(8):
            ones_ap = AP(v_sb, pr * 192 + 64,
                         [[NKT * VST, 128], [VST, NKT], [1, 64]])
            t_ones = p.pool("memset", ones_ap, 1.0)
        p.dma("gpsimd", bout_sb[:], bout_d[:, :], sem=pfx + "cst")
        if rope:
            p.dma("gpsimd", cos_sb[:], cos_d[:, :], sem=pfx + "cst")
            t_cst = p.dma("gpsimd", sin_sb[:], sin_d[:, :], sem=pfx + "cst")
        else:
            t_cst = (pfx + "cst", p.counts[pfx + "cst"])
        t_bout = t_cs = t_cst

        st = {"w_free": None,
              "tv": None,
              "kd": None,
              "qd0": None, "qd1": None,
              "rope_kd": None, "rope_qd0": None, "rope_qd1": None,
              "av_free": None,
              "y_rd": None,
              "wo_rd": [None, None],
              "x1_st0": None, "x1_ld0": None, "x1_st1": None,
              "x1_ld1": None, "norm_bc": None, "b5_rd": None}
        if not hasattr(p, "attn_hist"):
            p.attn_hist = {}
        hist = p.attn_hist.setdefault(pfx, {
            "exp_hist": {}, "av_rd": {}, "norm_free": [None, None]})
        exp_hist = hist["exp_hist"]
        av_rd = hist["av_rd"]
        norm_free = hist["norm_free"]

        bg = collections.deque()

        def emit_bg(n=1):
            for _ in range(n):
                if bg:
                    bg.popleft()()

        def drain_bg():
            while bg:
                bg.popleft()()

        def load_block(key):
            def th(key=key):
                tok = None
                for dc in range(NDC):
                    if rope and key < 3:
                        src = wqkv_d[dc * 128:(dc + 1) * 128,
                                     key * D:(key + 1) * D]
                    else:
                        src = {0: wq_d, 1: wk_d, 2: wv_d}[key]
                        src = src[dc * 128:(dc + 1) * 128, :]
                    tok = p.dma("gpsimd", wbuf[:, dc * D:(dc + 1) * D], src,
                                sem=pfx + "wld", waits=[st["w_free"]])
                st["wld"] = tok
            return th

        pro_slots = [(psX[0], 0), (psX[0], 512), (psX[1], 0), (psX[1], 512)]

        def pro_slot():
            q = st["pro_n"] % 4
            st["pro_n"] += 1
            tens, off = pro_slots[q]
            w = [st["pro_evac"][q]]
            if st["pro_fresh"][q]:
                w.append(exp_hist.get((q // 2, off // 512)))
                st["pro_fresh"][q] = False
            return tens[:, off:off + 512], (tens, off), w, q

        def v_group(sc, dv2):
            def th(sc=sc, dv2=dv2):
                ps, (tens, off), wslot, q = pro_slot()
                for dc in range(NDC):
                    w0 = ([st["wld"], st["y_tok"]] + wslot) if dc == 0 else []
                    t = p.mm(ps,
                             y_sb[:, dc * S + sc * 128:dc * S + (sc + 1) * 128],
                             wbuf[:, dc * D + dv2 * 512:dc * D + (dv2 + 1) * 512],
                             start=(dc == 0), stop=(dc == NDC - 1),
                             waits=w0, inc=(dc == NDC - 1))
                    if dc == NDC - 1:
                        st["w_free"] = t
                dst0 = sc * VST + 4 * dv2 * 192
                p.dve("tensor_copy",
                      AP(v_sb, dst0, [[NKT * VST, 128], [192, 4], [1, 64]]),
                      AP(tens, off, [[1024, 128], [128, 4], [1, 64]]),
                      waits=[t, t_ones, hist.get("avv")])
                tok = p.dve(
                    "tensor_copy",
                    AP(v_sb, dst0 + 128, [[NKT * VST, 128], [192, 4], [1, 64]]),
                    AP(tens, off + 64, [[1024, 128], [128, 4], [1, 64]]))
                st["tv"] = tok
                st["pro_evac"][q] = tok
            return th

        def qk_group(dst_sb, ec, j, key, bg7=False):
            def th(dst_sb=dst_sb, ec=ec, j=j, key=key, bg7=bg7):
                if bg7:
                    ps, wslot = BG7[:, 0:512], [p.s7]
                else:
                    ps, _, wslot, q = pro_slot()
                for dc in range(NDC):
                    w0 = ([st["wld"], st["y_tok"]] + wslot) if dc == 0 else []
                    t = p.mm(ps,
                             wbuf[:, dc * D + ec * 128:dc * D + (ec + 1) * 128],
                             y_sb[:, dc * S + j * SJ:dc * S + (j + 1) * SJ],
                             start=(dc == 0), stop=(dc == NDC - 1),
                             waits=w0, inc=(dc == NDC - 1))
                    if dc == NDC - 1:
                        st["w_free"] = t
                        st["y_rd"] = t
                tok = p.dve("tensor_copy",
                            dst_sb[:, ec * S + j * SJ:ec * S + (j + 1) * SJ],
                            ps, waits=[t])
                st[key] = tok
                if bg7:
                    p.s7 = tok
                else:
                    st["pro_evac"][q] = tok
            return th

        def rope_cols(src_sb, key, quarters):
            def th(src_sb=src_sb, key=key, quarters=quarters):
                last = None
                for c in range(4):
                    for qq in quarters:
                        o = qq * 512
                        QE = src_sb[:, c * S + o:c * S + o + 512]
                        QO = src_sb[:, (c + 4) * S + o:(c + 4) * S + o + 512]
                        cs = cos_sb[:, o:o + 512]
                        sn = sin_sb[:, o:o + 512]
                        p.pool("tensor_mul", rs0[:], QE, cs,
                               waits=[st[key], t_cs])
                        p.pool("tensor_mul", rs1[:], QE, sn)
                        p.pool("tensor_mul", QE, QO, sn)
                        p.pool("tensor_sub", QE, rs0[:], QE)
                        p.pool("tensor_mul", QO, QO, cs)
                        last = p.pool("tensor_add", QO, rs1[:], QO)
                st["rope_" + key] = last
            return th

        def ln_thunks(b, wait_y):
            out = []
            for j in range(NJ):
                def th(b=b, j=j):
                    def xrows(dc, b=b, j=j):
                        return x_d[b, dc * 128:(dc + 1) * 128,
                                   j * SJ:(j + 1) * SJ]
                    def yv(dc, j=j):
                        return y_sb[:, dc * S + j * SJ:dc * S + (j + 1) * SJ]
                    st["ln_prev"] = ln(xrows, yv,
                                       wait_loads=[wait_y()],
                                       prev=st.get("ln_prev"))
                    st["y_tok"] = st["ln_prev"]["apply"]
                out.append(th)
            return out

        def vaug(sc, h):
            off = sc * VST + (h // 2) * 192 + (64 if h % 2 else 0)
            return AP(v_sb, off, [[NKT * VST, 128], [1, 128]])

        avps = [psX[2][:, 0:512], psX[3][:, 0:512]]

        for b in range(nb):
            st["pro_n"] = 0
            st["pro_evac"] = [None] * 4
            st["pro_fresh"] = [True] * 4
            if b == 0:
                for th in ln_thunks(0, lambda: None):
                    th()
            load_block(1)()
            for j in range(NJ):
                for ec in range(NDC):
                    qk_group(k_sb, ec, j, "kd")()
                if rope:
                    rope_cols(k_sb, "kd", (j,))()
            load_block(2)()
            for sc in range(NKT):
                for dv2 in range(2):
                    v_group(sc, dv2)()
            load_block(0)()
            for j in (0, 1):
                for ec in range(NDC):
                    qk_group(q_sb, ec, j, "qd0")()
                if rope:
                    rope_cols(q_sb, "qd0", (j,))()
            if not rope:
                st["rope_kd"] = st["kd"]
                st["rope_qd0"] = st["qd0"]

            sc_tok = {}
            if post is not None and b > 0:
                y1_wl = [st["x1_st0"], st["x1_st1"]]
            for j in range(NJ):
                if post is not None and b > 0:
                    bg.append(y1_thunk(b - 1, j, y1_wl))
                if j == 0:
                    bg.append(load_block(0))
                    for jj in (2, 3):
                        for ec in range(NDC):
                            bg.append(qk_group(q_sb, ec, jj, "qd1", bg7=True))
                    if rope:
                        bg.append(rope_cols(q_sb, "qd1", (2, 3)))
                    else:
                        def setrd():
                            st["rope_qd1"] = st["qd1"]
                        bg.append(setrd)
                if j == 1 and b + 1 < nb:
                    for th in ln_thunks(b + 1, lambda: st["y_rd"]):
                        bg.append(th)
                if j == 2:
                    drain_bg()
                for g2 in range(8):
                    heads = [2 * g2, 2 * g2 + 1]
                    epair = {}
                    for kt in range(16):
                        half = kt % 2
                        for m in range(2):
                            h = heads[m]
                            w0 = [exp_hist.get((m, half))]
                            if g2 == 0 and kt < 2:
                                w0 += [st["rope_kd"],
                                       st["rope_qd%d" % (j // 2)]]
                                if b == 0 and j == 0:
                                    w0.append(st["tv"])
                            ps = psX[m][:, half * 512:half * 512 + 512]
                            if rope:
                                strip = 32 * (h % 4)
                                ch = h // 4
                                p.mm(ps, k_sb[strip:strip + 32,
                                              ch * S + kt * 128:ch * S + (kt + 1) * 128],
                                     q_sb[strip:strip + 32,
                                          ch * S + j * SJ:ch * S + (j + 1) * SJ],
                                     start=True, stop=False, tp=(strip, 0),
                                     waits=w0, sgc=True)
                                t = p.mm(ps, k_sb[strip:strip + 32,
                                                  (ch + 4) * S + kt * 128:(ch + 4) * S + (kt + 1) * 128],
                                         q_sb[strip:strip + 32,
                                              (ch + 4) * S + j * SJ:(ch + 4) * S + (j + 1) * SJ],
                                         start=False, stop=True, tp=(strip, 0),
                                         inc=True, sgc=True)
                            else:
                                strip = 64 * (h % 2)
                                ch = h // 2
                                t = p.mm(ps, k_sb[strip:strip + 64,
                                                  ch * S + kt * 128:ch * S + (kt + 1) * 128],
                                         q_sb[strip:strip + 64,
                                              ch * S + j * SJ:ch * S + (j + 1) * SJ],
                                         start=True, stop=True, tp=(strip, 0),
                                         waits=w0, inc=True, sgc=True)
                            sc_tok[(m, half)] = t
                        for m in range(2):
                            slot = ((kt // 2 % 2) * 2 + m) * 1024 + half * 512
                            e = p.act(stage[:, slot:slot + 512],
                                      psX[m][:, half * 512:half * 512 + 512],
                                      AF.Exp, bias=ebias_sb[:], scale=0.125,
                                      waits=[sc_tok[(m, half)],
                                             av_rd.get((kt % 4, m))])
                            exp_hist[(m, half)] = e
                            epair[(m, kt)] = e
                        if kt >= 2:
                            ka = kt - 2
                            for m in range(2):
                                slot = ((ka // 2 % 2) * 2 + m) * 1024 + (ka % 2) * 512
                                w0 = [epair.pop((m, ka))]
                                if ka == 0:
                                    w0.append(norm_free[m])
                                tav = p.mm(avps[m], vaug(ka, heads[m]),
                                           stage[:, slot:slot + 512],
                                           start=(ka == 0), stop=False,
                                           waits=w0, inc=True, sgc=True)
                                av_rd[(ka % 4, m)] = tav
                        if half == 0:
                            emit_bg(1)
                    for ka in (14, 15):
                        for m in range(2):
                            slot = ((ka // 2 % 2) * 2 + m) * 1024 + (ka % 2) * 512
                            tav = p.mm(avps[m], vaug(ka, heads[m]),
                                       stage[:, slot:slot + 512],
                                       start=False, stop=(ka == 15),
                                       waits=[epair.pop((m, ka))], inc=True,
                                       sgc=True)
                            av_rd[(ka % 4, m)] = tav
                    t2s = []
                    for m in range(2):
                        h = heads[m]
                        dn_r = 64 if h % 2 == 0 else 0
                        t1 = p.dve("tensor_copy", recip_sb[:],
                                   avps[m][dn_r:dn_r + 1, 0:512],
                                   waits=[av_rd[(3, m)], exp_hist[(m, 1)]] + t2s)
                        t2 = p.mm(psX[2][64 * m:64 * m + 64, 512:1024],
                                  ones64f[0:1, :], recip_sb[:],
                                  start=True, stop=True,
                                  waits=[t1, st["norm_bc"], st["b5_rd"]],
                                  tp=(0, 64 * m) if m else None, inc=True)
                        t2s.append(t2)
                    tln = p.act(rb_sb[:], psX[2][:, 512:1024], AF.Ln,
                                waits=t2s)
                    st["norm_bc"] = tln
                    tex = p.act(rb_bf[:], rb_sb[:], AF.Exp, scale=-1.0,
                                waits=[tln])
                    for m in range(2):
                        h = heads[m]
                        av_r0 = 0 if h % 2 == 0 else 64
                        av_written = p.dve(
                            "tensor_mul",
                            av_sb[64 * (h % 2):64 * (h % 2) + 64,
                                  (h // 2) * SJ:(h // 2 + 1) * SJ],
                            avps[m][av_r0:av_r0 + 64, 0:512],
                            rb_bf[64 * m:64 * m + 64, :],
                            waits=[tex, st["av_free"]])
                        norm_free[m] = av_written
                hist["avv"] = av_rd[(3, 1)]
                op_banks = [psX[2][:, 512:1024], BG7]
                x1ts = [x1t[:], x1t[:]]
                for ec in range(NDC):
                    sl = ec % 2
                    bank = op_banks[sl]
                    tw = p.dma("gpsimd", wo_ec[sl][:],
                               wout_d[:, ec * 128:(ec + 1) * 128].rearrange(
                                   "(a q) c -> q a c", q=128),
                               sem=pfx + "wo%d" % sl, waits=[st["wo_rd"][sl]])
                    for dc in range(NDC):
                        if dc == 0:
                            w0 = [tw, av_written,
                                  (p.s7 if sl else st["norm_bc"]),
                                  (None if sl else st["b5_rd"])]
                        else:
                            w0 = []
                        t = p.mm(bank[:, 0:512],
                                 wo_ec[sl][:, dc * 128:(dc + 1) * 128],
                                 av_sb[:, dc * SJ:(dc + 1) * SJ],
                                 start=(dc == 0), stop=(dc == NDC - 1),
                                 waits=w0, inc=(dc == NDC - 1))
                    st["wo_rd"][sl] = t
                    st["av_free"] = t
                    tx = p.dma("sync", x1ts[sl],
                               x_d[b, ec * 128:(ec + 1) * 128,
                                   j * SJ:(j + 1) * SJ],
                               sem=pfx + "sx%d" % sl,
                               waits=[st["x1_st0"], st["x1_st1"],
                                      st["x1_ld0"], st["x1_ld1"]])
                    st["x1_ld%d" % sl] = tx
                    td = p.dve("scalar_tensor_tensor", x1ts[sl], bank[:, 0:512],
                               bout_sb[:, ec:ec + 1], x1ts[sl], AL.add, AL.add,
                               waits=[t, t_bout, tx])
                    if sl:
                        p.s7 = td
                    else:
                        st["b5_rd"] = td
                    st["x1_st%d" % sl] = p.dma(
                        "sync",
                        x1_d[b, ec * 128:(ec + 1) * 128, j * SJ:(j + 1) * SJ],
                        x1ts[sl], sem=pfx + "stx%d" % sl, waits=[td])
            drain_bg()
            if post is not None and b > 0:
                y1_coll(b - 1)
        if post is not None:
            wl = [st["x1_st0"], st["x1_st1"]]
            for j in range(NJ):
                y1_thunk(nb - 1, j, wl)()
            y1_coll(nb - 1)


def finish(nc, p):
    p.emit("sync", lambda e: e.nop(),
           waits=[(name, cnt) for name, cnt in p.counts.items() if cnt])
    with nc.Block() as block:
        p.replay(block)


def build_attn_layer(rope, nb=B_PER):
    nc = bass.Bass("TRN2", target_bir_lowering=False)
    dr = {"x": nc.dram_tensor("x", [nb, D, S], F32, kind="ExternalInput").ap()}
    if rope:
        dr["wqkvT"] = nc.dram_tensor("wqkvT", [D, 3 * D], F32, kind="ExternalInput").ap()
    else:
        for nm in ("Wq", "Wk", "Wv"):
            dr[nm] = nc.dram_tensor(nm, [D, D], F32, kind="ExternalInput").ap()
    dr["woutT"] = nc.dram_tensor("woutT", [D, D], F32, kind="ExternalInput").ap()
    dr["bout_rs"] = nc.dram_tensor("bout_rs", [128, NDC], F32, kind="ExternalInput").ap()
    if rope:
        dr["cos_rep"] = nc.dram_tensor("cos_rep", [128, S], F32, kind="ExternalInput").ap()
        dr["sin_rep"] = nc.dram_tensor("sin_rep", [128, S], F32, kind="ExternalInput").ap()
    dr["x_out"] = nc.dram_tensor("x1", [nb, D, S], F32, kind="ExternalOutput").ap()
    with ExitStack() as octx:
        psX = [pst(nc, octx, f"ps{i}", [128, 1024]) for i in range(4)]
        p = Plan(nc, octx)
        emit_attn(nc, p, psX, dr, rope, nb)
        finish(nc, p)
    return nc


def emit_ffn0_ln(nc, p, psX, x_d, y_hbm, nt=T, npos=POS, pfx="f0l"):
    NP8 = NDC * npos
    with ExitStack() as ctx:
        ybuf = [sbt(nc, ctx, f"ybuf{i}", [128, NP8], BF16) for i in range(2)]
        ln = make_ln(nc, ctx, p, npos, ps_stat=psX[3][0:64, 0:npos],
                     ps_bc=psX[3][:, 512:512 + npos], pfx=pfx)
        ln_prev = None
        y_str = [None, None]
        for t in range(nt):
            sl = t % 2
            def xrows(dc, t=t):
                return x_d[t, dc * 128:(dc + 1) * 128, :]
            def yv(dc, sl=sl):
                return ybuf[sl][:, dc * npos:(dc + 1) * npos]
            ln_prev = ln(xrows, yv, wait_loads=[y_str[sl]], prev=ln_prev)
            ydst = y_hbm.ap()[t].rearrange("(a b) c -> b a c", b=128)
            y_str[sl] = p.dma("gpsimd", ydst, ybuf[sl][:],
                              sem=pfx + "yst%d" % sl, waits=[ln_prev["apply"]])
    return [(pfx + "yst%d" % i, p.counts.get(pfx + "yst%d" % i, 0))
            for i in range(2)]


def emit_ffn0_core(nc, p, psX, dr, y_ready, nt=T, npos=POS, pfx="f0",
                   ygate=None):
    NP8 = NDC * npos
    wg_d, bias_d = dr["wg"], dr["bias"]
    ysrc = dr["ysrc"]
    with ExitStack() as ctx:
        wsb = {nm: sbt(nc, ctx, "w_" + nm, [128, 8 * D], BF16)
               for nm in ("wf", "wi", "wc", "uf", "ui", "uc")}
        c_sb = [sbt(nc, ctx, f"c_sb{i}", [128, NP8], F32) for i in range(2)]
        cbf = [sbt(nc, ctx, f"cbf{i}", [128, NP8], BF16) for i in range(2)]
        fga = {g: sbt(nc, ctx, f"g_{g}", [128, NP8], F32) for g in "fig"}
        tmp2 = [sbt(nc, ctx, f"scnt{i}", [128, npos], F32) for i in range(2)]
        cuo_sb = sbt(nc, ctx, "cuo_sb", [128, NP8], F32)
        ones_t = sbt(nc, ctx, "ones_t", [128, npos], F32)
        bsb = {nm: sbt(nc, ctx, "sb_" + nm, [128, NDC], F32)
               for nm in ("bf", "bi", "bc", "bo", "inc")}
        hbufs = [sbt(nc, ctx, f"hbuf{i}", [128, npos], F32) for i in range(2)]
        xfb = [sbt(nc, ctx, f"xfb{i}", [128, npos], F32) for i in range(2)]
        x2b = [sbt(nc, ctx, f"x2b{i}", [128, npos], F32) for i in range(2)]
        ylg = [sbt(nc, ctx, f"ylg{i}", [128, NP8], BF16) for i in range(2)]
        yl = [sbt(nc, ctx, f"yl{i}", [128, NP8], BF16) for i in range(2)]

        slots = [psX[0][:, 0:npos], psX[0][:, 512:512 + npos],
                 psX[1][:, 0:npos], psX[1][:, 512:512 + npos],
                 psX[2][:, 512:512 + npos]]
        NSL = len(slots)

        t_one = p.dve("memset", ones_t[:], 1.0)
        t_cst = None
        for nm in ("bf", "bi", "bc", "bo", "inc"):
            t_cst = p.dma("sync", bsb[nm][:], bias_d[nm][:, :],
                          sem=pfx + "cst")
        t_w = None
        for nm in ("wf", "wi", "wc", "uf", "ui", "uc"):
            for dc in range(NDC):
                t_w = p.dma("gpsimd", wsb[nm][:, dc * D:(dc + 1) * D],
                            wg_d[nm][dc * 128:(dc + 1) * 128, :],
                            sem=pfx + "wld")
        if dr.get("pre_colls"):
            dr["pre_colls"]()

        t_cbf = None
        for mc in range(NDC):
            p.dve("tensor_scalar_mul", c_sb[0][:, mc * npos:(mc + 1) * npos],
                  ones_t[:, :], bsb["inc"][:, mc:mc + 1],
                  waits=[t_cst, t_one])
            t_cbf = p.dve("tensor_copy", cbf[0][:, mc * npos:(mc + 1) * npos],
                          c_sb[0][:, mc * npos:(mc + 1) * npos])

        gate_bias = {"f": "bf", "i": "bi", "c": "bc"}
        gate_w = {"f": ("wf", "uf"), "i": ("wi", "ui"), "c": ("wc", "uc")}
        gate_buf = {"f": "f", "i": "i", "c": "g"}

        yg_free = [None, None]
        upd_tok = [None] * NDC
        slot_act = [None] * len(slots)
        nps = 0
        gmm_last = None
        tylg = [None, None]
        if ygate is None:
            ygate = lambda t: (list(y_ready) if t == 0 else [])
        tylg[0] = p.dma("sync", ylg[0][:], ysrc(0), sem=pfx + "yg0",
                        waits=list(y_ready) + ygate(0))
        for t in range(nt):
            sl = t % 2
            if t + 1 < nt:
                tylg[(t + 1) % 2] = p.dma(
                    "sync", ylg[(t + 1) % 2][:], ysrc(t + 1),
                    sem=pfx + "yg%d" % ((t + 1) % 2),
                    waits=[yg_free[(t + 1) % 2]] + ygate(t + 1))
            cp_ = t % 2
            for mc in range(NDC):
                sig_toks = {}
                for g in "fic":
                    wn, un = gate_w[g]
                    q = nps % NSL
                    ps = slots[q]
                    nps += 1
                    for dc in range(NDC):
                        w0 = []
                        if dc == 0:
                            w0 = [t_w, t_cbf, tylg[sl], slot_act[q]]
                        p.mm(ps, wsb[un][:, dc * D + mc * 128:dc * D + (mc + 1) * 128],
                             cbf[cp_][:, dc * npos:(dc + 1) * npos],
                             start=(dc == 0), stop=False, waits=w0, sgc=True)
                    for dc in range(NDC):
                        gmm_last = p.mm(
                            ps, wsb[wn][:, dc * D + mc * 128:dc * D + (mc + 1) * 128],
                            ylg[sl][:, dc * npos:(dc + 1) * npos],
                            start=False, stop=(dc == NDC - 1), sgc=True,
                            inc=(dc == NDC - 1))
                    fn_act = AF.Sigmoid if g in "fi" else AF.Tanh
                    sa = p.act(fga[gate_buf[g]][:, mc * npos:(mc + 1) * npos],
                               ps, fn_act, bias=bsb[gate_bias[g]][:, mc:mc + 1],
                               waits=[gmm_last, upd_tok[mc]])
                    slot_act[q] = sa
                    sig_toks[g] = sa
                cold = c_sb[cp_][:, mc * npos:(mc + 1) * npos]
                cnew = c_sb[1 - cp_][:, mc * npos:(mc + 1) * npos]
                p.dve("tensor_mul", tmp2[0][:], fga["f"][:, mc * npos:(mc + 1) * npos],
                      cold, waits=[sig_toks["f"]])
                p.dve("tensor_mul", tmp2[1][:], fga["i"][:, mc * npos:(mc + 1) * npos],
                      fga["g"][:, mc * npos:(mc + 1) * npos],
                      waits=[sig_toks["i"], sig_toks["c"]])
                p.dve("tensor_add", cnew, tmp2[0][:], tmp2[1][:])
                t_cbf = p.dve("tensor_copy",
                              cbf[1 - cp_][:, mc * npos:(mc + 1) * npos], cnew)
                upd_tok[mc] = t_cbf
            yg_free[sl] = gmm_last

        t_w2 = None
        for nm, key in (("wf", "wo"), ("wi", "uo")):
            for dc in range(NDC):
                t_w2 = p.dma("gpsimd", wsb[nm][:, dc * D:(dc + 1) * D],
                             wg_d[key][dc * 128:(dc + 1) * 128, :],
                             sem=pfx + "wld", waits=[gmm_last])
        for mc in range(NDC):
            q = nps % NSL
            ps = slots[q]
            nps += 1
            for dc in range(NDC):
                w0 = []
                if dc == 0:
                    w0 = [t_w2, t_cbf, slot_act[q]]
                tmm = p.mm(ps, wsb["wi"][:, dc * D + mc * 128:dc * D + (mc + 1) * 128],
                           cbf[nt % 2][:, dc * npos:(dc + 1) * npos],
                           start=(dc == 0), stop=(dc == NDC - 1), waits=w0,
                           inc=(dc == NDC - 1))
            te = p.dve("tensor_copy", cuo_sb[:, mc * npos:(mc + 1) * npos], ps,
                       waits=[tmm])
            slot_act[q] = te
        cuo_tok = te
        yl_free = [None, None]
        x2_st = [None, None]
        x2_ld = [None, None]
        hadd = [None, None]
        h_st = [None, None]
        resid = "x2" in dr
        collh = dr.get("collh")
        x2sp = dr.get("x2sp")
        if x2sp is not None:
            hbx = [sbt(nc, ctx, f"x2h{i}", [128, SJ], BF16) for i in range(2)]
            xbx = [sbt(nc, ctx, f"x2x{i}", [128, SJ], F32) for i in range(2)]
            obx = [sbt(nc, ctx, f"x2o{i}", [128, SJ], F32) for i in range(2)]
            xst = {"st": [None, None], "ta": [None, None], "n": 0}

            def emit_x2_part(k, part):
                gate = x2sp["gate"](k) if part == 0 else []
                h_ap = x2sp["h_out"](k).ap()
                for n4 in range(4):
                    ec, j = divmod(part * 4 + n4, NJ)
                    sl = xst["n"] % 2
                    xst["n"] += 1
                    th = []
                    for hh in (0, 1):
                        row0 = (2 * j + hh) * D + ec * 128
                        tt = p.dma(
                            "sync", hbx[sl][:, hh * POS:(hh + 1) * POS],
                            h_ap[row0:row0 + 128, :],
                            sem=pfx + "xh%d%d" % (sl, hh),
                            waits=[xst["ta"][sl]] + (gate if n4 == 0 else []))
                        th.append(tt)
                    tx = p.dma("sync", xbx[sl][:],
                               x2sp["x1_ap"][k, ec * 128:(ec + 1) * 128,
                                             j * SJ:(j + 1) * SJ],
                               sem=pfx + "xx%d" % sl,
                               waits=[xst["ta"][sl]])
                    ta = p.dve("tensor_add", obx[sl][:], xbx[sl][:],
                               hbx[sl][:], waits=th + [tx, xst["st"][sl]])
                    xst["ta"][sl] = ta
                    xst["st"][sl] = p.dma(
                        "sync",
                        x2sp["x2_ap"][k, ec * 128:(ec + 1) * 128,
                                      j * SJ:(j + 1) * SJ],
                        obx[sl][:], sem=pfx + "xs%d" % sl, waits=[ta])
            x2q = []
        for t in range(nt):
            sl = t % 2
            tyl = p.dma("sync", yl[sl][:], ysrc(t), sem=pfx + "ylb%d" % sl,
                        waits=[yl_free[sl]])
            for mc in range(NDC):
                q = nps % NSL
                ps = slots[q]
                nps += 1
                for dc in range(NDC):
                    w0 = []
                    if dc == 0:
                        w0 = [tyl, cuo_tok, slot_act[q]]
                    tmm = p.mm(ps, wsb["wf"][:, dc * D + mc * 128:dc * D + (mc + 1) * 128],
                               yl[sl][:, dc * npos:(dc + 1) * npos],
                               start=(dc == 0), stop=(dc == NDC - 1), waits=w0,
                               inc=(dc == NDC - 1))
                hsl = mc % 2
                tadd = p.dve("tensor_add", ps, ps,
                             cuo_sb[:, mc * npos:(mc + 1) * npos],
                             waits=[tmm, hadd[hsl] if resid else h_st[hsl]])
                if resid:
                    th = p.act(hbufs[hsl][:], ps, AF.Tanh,
                               bias=bsb["bo"][:, mc:mc + 1],
                               waits=[tadd, hadd[hsl]])
                    slot_act[q] = th
                    tx = p.dma("sync", xfb[hsl][:],
                               dr["x1s"][t, mc * 128:(mc + 1) * 128, :],
                               sem=pfx + "hx%d" % hsl,
                               waits=[x2_ld[hsl], hadd[hsl]])
                    x2_ld[hsl] = tx
                    ta = p.dve("tensor_add", x2b[hsl][:], xfb[hsl][:],
                               hbufs[hsl][:], waits=[th, tx, x2_st[hsl]])
                    hadd[hsl] = ta
                    x2_st[hsl] = p.dma("gpsimd",
                                       dr["x2"][t, mc * 128:(mc + 1) * 128, :],
                                       x2b[hsl][:], sem=pfx + "hs%d" % hsl,
                                       waits=[ta])
                else:
                    hb = cbf[0] if False else None
                    th = p.act(ylg[hsl][:, 0:npos], ps, AF.Tanh,
                               bias=bsb["bo"][:, mc:mc + 1],
                               waits=[tadd, h_st[hsl]])
                    slot_act[q] = th
                    h_st[hsl] = p.dma("sync", dr["hdst"](t, mc),
                                      ylg[hsl][:, 0:npos],
                                      sem=pfx + "hs%d" % hsl, waits=[th])
            yl_free[sl] = tmm
            if collh is not None and t % 8 == 7:
                collh(t // 8, [h_st[0], h_st[1]])
            if x2sp is not None:
                if t >= 10 and (t - 10) % 8 == 0 and (t - 10) // 8 < 3:
                    x2q.extend(((t - 10) // 8, pp) for pp in range(8))
                for _ in range(2):
                    if x2q:
                        emit_x2_part(*x2q.pop(0))
        if x2sp is not None:
            while x2q:
                emit_x2_part(*x2q.pop(0))
            for pp in range(8):
                emit_x2_part(nt // 8 - 1, pp)
    return [(pfx + "hs%d" % i, p.counts.get(pfx + "hs%d" % i, 0))
            for i in range(2)]


def build_l0_ffn(npos=POS, nt=T):
    nc = bass.Bass("TRN2", target_bir_lowering=False)
    x_d = nc.dram_tensor("x1s", [nt, D, npos], F32, kind="ExternalInput").ap()
    wg_d = {}
    for nm, d in (("wf", "wfT"), ("uf", "ufT"), ("wi", "wiT"), ("ui", "uiT"),
                  ("wc", "wcT"), ("uc", "ucT"), ("wo", "woT"), ("uo", "uoT")):
        wg_d[nm] = nc.dram_tensor(d, [D, D], F32, kind="ExternalInput").ap()
    bias_d = {}
    for nm, d in (("bf", "bf_rs"), ("bi", "bi_rs"), ("bc", "bc_rs"),
                  ("bo", "bo_rs"), ("inc", "inc_rs")):
        bias_d[nm] = nc.dram_tensor(d, [128, NDC], F32, kind="ExternalInput").ap()
    x2_d = nc.dram_tensor("x2", [nt, D, npos], F32, kind="ExternalOutput").ap()
    y_hbm = nc.dram_tensor("y2_hbm", [nt, D, npos], BF16, kind="Internal")
    with ExitStack() as octx:
        psX = [pst(nc, octx, f"ps{i}", [128, 1024]) for i in range(4)]
        p = Plan(nc, octx)
        y_ready = emit_ffn0_ln(nc, p, psX, x_d, y_hbm, nt, npos)
        def ysrc(t):
            return y_hbm.ap()[t].rearrange("(a b) c -> b a c", b=128)
        dr = {"wg": wg_d, "bias": bias_d, "ysrc": ysrc, "x1s": x_d, "x2": x2_d}
        emit_ffn0_core(nc, p, psX, dr, y_ready, nt, npos)
        finish(nc, p)
    return nc


def prep_l0_ffn(inputs):
    return {
        "wfT": np.ascontiguousarray(inputs["L0_ffn_wf"].T),
        "ufT": np.ascontiguousarray(inputs["L0_ffn_uf"].T),
        "wiT": np.ascontiguousarray(inputs["L0_ffn_wi"].T),
        "uiT": np.ascontiguousarray(inputs["L0_ffn_ui"].T),
        "wcT": np.ascontiguousarray(inputs["L0_ffn_wc"].T),
        "ucT": np.ascontiguousarray(inputs["L0_ffn_uc"].T),
        "woT": np.ascontiguousarray(inputs["L0_ffn_wo"].T),
        "uoT": np.ascontiguousarray(inputs["L0_ffn_uo"].T),
        "bf_rs": _rs(inputs["L0_ffn_bf"]), "bi_rs": _rs(inputs["L0_ffn_bi"]),
        "bc_rs": _rs(inputs["L0_ffn_bc"]), "bo_rs": _rs(inputs["L0_ffn_bo"]),
        "inc_rs": _rs(inputs["L0_ffn_in_c"]),
    }


def run_l0_ffn(inputs, X1f):
    nc = build_l0_ffn()
    common = prep_l0_ffn(inputs)
    in_maps = []
    for i in range(NCORES):
        m = dict(common)
        m["x1s"] = np.ascontiguousarray(X1f[:, :, POS * i:POS * (i + 1)])
        in_maps.append(m)
    res = run_bass_kernel_spmd(nc, in_maps, core_ids=list(range(NCORES)))
    out = np.empty((T, D, S), np.float32)
    for i in range(NCORES):
        out[:, :, POS * i:POS * (i + 1)] = res.results[i]["x2"]
    return out


def emit_ffn1(nc, p, psX, dr, nb=B_PER, pfx="f1"):
    x_d = dr["x"]
    wr_d, wk_d, wv_d = dr["WrT"], dr["WkT"], dr["WvT"]
    mur_d, muk_d = dr["mur_rs"], dr["muk_rs"]
    x4_d = dr["x_out"]

    with ExitStack() as ctx:
        wsb = {nm: sbt(nc, ctx, "w_" + nm, [128, 8 * D], BF16)
               for nm in ("wr", "wk", "wv")}
        y_sb = sbt(nc, ctx, "y_sb", [128, NDC * S], BF16)
        xr_sb = sbt(nc, ctx, "xr_sb", [128, NDC * SJ], BF16)
        xk_sb = sbt(nc, ctx, "xk_sb", [128, NDC * SJ], BF16)
        dsc = sbt(nc, ctx, "dsc", [128, SJ], BF16)
        r_sb = sbt(nc, ctx, "r_sb", [128, NDC * SJ], BF16)
        kact = sbt(nc, ctx, "kact", [128, NDC * SJ], BF16)
        krel = sbt(nc, ctx, "krel", [128, SJ], BF16)
        musb = {"r": sbt(nc, ctx, "mur_sb", [128, NDC], F32),
                "k": sbt(nc, ctx, "muk_sb", [128, NDC], F32)}
        tmpv = [sbt(nc, ctx, f"tmpv{i}", [128, SJ], F32) for i in range(2)]
        xfb = [sbt(nc, ctx, f"xfb{i}", [128, SJ], F32) for i in range(2)]
        x4b = [sbt(nc, ctx, f"x4b{i}", [128, SJ], F32) for i in range(2)]

        slots = [psX[0][:, 0:512], psX[0][:, 512:1024],
                 psX[1][:, 0:512], psX[1][:, 512:1024],
                 psX[2][:, 512:1024]]
        NSL = len(slots)
        ln = make_ln(nc, ctx, p, SJ, ps_stat=psX[3][0:64, 0:512],
                     ps_bc=psX[3][:, 512:1024], pfx=pfx)

        t_cst = p.dma("gpsimd", musb["r"][:], mur_d[:, :], sem=pfx + "cst")
        t_cst = p.dma("gpsimd", musb["k"][:], muk_d[:, :], sem=pfx + "cst")
        t_w = None
        for nm, d in (("wr", wr_d), ("wk", wk_d), ("wv", wv_d)):
            for dc in range(NDC):
                t_w = p.dma("gpsimd", wsb[nm][:, dc * D:(dc + 1) * D],
                            d[dc * 128:(dc + 1) * 128, :], sem=pfx + "wld")

        ln_prev = None
        slot_rd = [None] * NSL
        xmix_free = None
        r_free = None
        kact_free = None
        vres = [None, None]
        vres_act = [None, None]
        x4_st = [None, None]
        x4_ld = [None, None]
        y_stats_free = None
        for b in range(nb):
            for j in range(NJ):
                def xrows(dc, b=b, j=j):
                    return x_d[b, dc * 128:(dc + 1) * 128, j * SJ:(j + 1) * SJ]
                def yv(dc, b=b, j=j):
                    return y_sb[:, dc * S + j * SJ:dc * S + (j + 1) * SJ]
                ln_prev = ln(xrows, yv, wait_loads=[y_stats_free], prev=ln_prev)
            y_tok = ln_prev["apply"]
            for j in range(NJ):
                tmix = None
                for dc in range(NDC):
                    yj = y_sb[:, dc * S + j * SJ:dc * S + (j + 1) * SJ]
                    for side, dst in (("r", xr_sb), ("k", xk_sb)):
                        mu = musb[side][:, dc:dc + 1]
                        o = dc * SJ
                        if j == 0:
                            yp = y_sb[:, dc * S + j * SJ:dc * S + (j + 1) * SJ - 1]
                            p.dve("tensor_sub", dsc[:, 1:SJ],
                                  y_sb[:, dc * S + 1:dc * S + SJ], yp,
                                  waits=[y_tok, xmix_free])
                            p.dve("scalar_tensor_tensor", dst[:, o + 1:o + SJ],
                                  dsc[:, 1:SJ], mu, yp, AL.mult, AL.add,
                                  waits=[t_cst])
                            tmix = p.dve("tensor_scalar_mul", dst[:, o:o + 1],
                                         yj[:, 0:1], mu)
                        else:
                            yp = y_sb[:, dc * S + j * SJ - 1:dc * S + (j + 1) * SJ - 1]
                            p.dve("tensor_sub", dsc[:], yj, yp,
                                  waits=[y_tok, xmix_free])
                            tmix = p.dve("scalar_tensor_tensor", dst[:, o:o + SJ],
                                         dsc[:], mu, yp, AL.mult, AL.add,
                                         waits=[t_cst])
                nps = 0
                rk_tok = {}
                for side, src, dst in (("r", xr_sb, r_sb), ("k", xk_sb, kact)):
                    wn = "wr" if side == "r" else "wk"
                    for oc in range(NDC):
                        q = nps % NSL
                        ps = slots[q]
                        nps += 1
                        for dc in range(NDC):
                            w0 = []
                            if dc == 0:
                                w0 = [t_w, tmix, slot_rd[q]]
                            t = p.mm(ps, wsb[wn][:, dc * D + oc * 128:dc * D + (oc + 1) * 128],
                                     src[:, dc * SJ:(dc + 1) * SJ],
                                     start=(dc == 0), stop=(dc == NDC - 1),
                                     waits=w0, inc=(dc == NDC - 1))
                        if side == "r":
                            tk = p.act(dst[:, oc * SJ:(oc + 1) * SJ], ps,
                                       AF.Sigmoid, waits=[t, r_free])
                        else:
                            p.act(krel[:], ps, AF.Relu, waits=[t])
                            tk = p.act(dst[:, oc * SJ:(oc + 1) * SJ], krel[:],
                                       AF.Square, waits=[kact_free])
                        slot_rd[q] = tk
                        rk_tok[(side, oc)] = tk
                xmix_free = t
                for o2 in range(NDC):
                    q = nps % NSL
                    ps = slots[q]
                    nps += 1
                    for dc in range(NDC):
                        w0 = []
                        if dc == 0:
                            w0 = [rk_tok[("k", NDC - 1)], slot_rd[q]]
                        t = p.mm(ps, wsb["wv"][:, dc * D + o2 * 128:dc * D + (o2 + 1) * 128],
                                 kact[:, dc * SJ:(dc + 1) * SJ],
                                 start=(dc == 0), stop=(dc == NDC - 1),
                                 waits=w0, inc=(dc == NDC - 1))
                    kact_free = t
                    sl = o2 % 2
                    tv = p.dve("tensor_mul", tmpv[sl][:], ps,
                               r_sb[:, o2 * SJ:(o2 + 1) * SJ],
                               waits=[t, rk_tok[("r", o2)], vres[sl]])
                    slot_rd[q] = tv
                    r_free = tv
                    tx = p.dma("sync", xfb[sl][:],
                               x_d[b, o2 * 128:(o2 + 1) * 128, j * SJ:(j + 1) * SJ],
                               sem=pfx + "sx%d" % sl, waits=[x4_ld[sl], vres[sl]])
                    x4_ld[sl] = tx
                    ta = p.dve("tensor_add", x4b[sl][:], xfb[sl][:], tmpv[sl][:],
                               waits=[tx, x4_st[sl]])
                    vres[sl] = ta
                    x4_st[sl] = p.dma("gpsimd",
                                      x4_d[b, o2 * 128:(o2 + 1) * 128,
                                           j * SJ:(j + 1) * SJ],
                                      x4b[sl][:], sem=pfx + "stx%d" % sl, waits=[ta])
            y_stats_free = xmix_free


def build_l1_ffn(nb=B_PER):
    nc = bass.Bass("TRN2", target_bir_lowering=False)
    dr = {"x": nc.dram_tensor("x3", [nb, D, S], F32, kind="ExternalInput").ap(),
          "WrT": nc.dram_tensor("WrT", [D, D], F32, kind="ExternalInput").ap(),
          "WkT": nc.dram_tensor("WkT", [D, D], F32, kind="ExternalInput").ap(),
          "WvT": nc.dram_tensor("WvT", [D, D], F32, kind="ExternalInput").ap(),
          "mur_rs": nc.dram_tensor("mur_rs", [128, NDC], F32, kind="ExternalInput").ap(),
          "muk_rs": nc.dram_tensor("muk_rs", [128, NDC], F32, kind="ExternalInput").ap(),
          "x_out": nc.dram_tensor("x4", [nb, D, S], F32, kind="ExternalOutput").ap()}
    with ExitStack() as octx:
        psX = [pst(nc, octx, f"ps{i}", [128, 1024]) for i in range(4)]
        p = Plan(nc, octx)
        emit_ffn1(nc, p, psX, dr, nb)
        finish(nc, p)
    return nc


def prep_l1_ffn(inputs):
    return {"WrT": np.ascontiguousarray(inputs["L1_ffn_W_r"].T),
            "WkT": np.ascontiguousarray(inputs["L1_ffn_W_k"].T),
            "WvT": np.ascontiguousarray(inputs["L1_ffn_W_v"].T),
            "mur_rs": _rs(inputs["L1_ffn_mu_r"]),
            "muk_rs": _rs(inputs["L1_ffn_mu_k"])}


def run_l1_ffn(inputs, X3f):
    nc = build_l1_ffn()
    common = prep_l1_ffn(inputs)
    in_maps = []
    for i in range(NCORES):
        m = dict(common)
        m["x3"] = np.ascontiguousarray(X3f[B_PER * i:B_PER * (i + 1)])
        in_maps.append(m)
    res = run_bass_kernel_spmd(nc, in_maps, core_ids=list(range(NCORES)))
    return np.concatenate([r["x4"] for r in res.results], axis=0)


def emit_ln_y1(nc, p, psX, x1_ap, stage_fn, coll_fn, nb=B_PER, pfx="y1"):
    with ExitStack() as ctx:
        lns = [make_ln(nc, ctx, p, SJ, ps_stat=psX[3 - c][0:64, 0:512],
                       ps_bc=psX[3 - c][:, 512:1024], pfx=pfx + "ab"[c],
                       use_s7=(c == 0)) for c in range(2)]
        ybuf = [sbt(nc, ctx, f"y1buf{c}", [128, 2 * SJ], BF16)
                for c in range(2)]
        prevs = [None, None]
        war = {}
        for b in range(nb):
            for j in range(NJ):
                c = j % 2
                def xrows(dc, b=b, j=j):
                    return x1_ap[b, dc * 128:(dc + 1) * 128,
                                 j * SJ:(j + 1) * SJ]
                def yv(dc, c=c):
                    return ybuf[c][:, (dc % 2) * SJ:(dc % 2 + 1) * SJ]
                def ywar(dc, c=c):
                    return [war.get((c, dc % 2, 0)), war.get((c, dc % 2, 1))]
                def ydone(dc, ty, b=b, j=j, c=c):
                    sl = dc % 2
                    for r in (0, 1):
                        tok = p.dma(
                            "sync", stage_fn(b, j, dc, r),
                            ybuf[c][:, sl * SJ + r * POS:
                                    sl * SJ + (r + 1) * POS],
                            sem=pfx + "s%d%d%d" % (c, sl, r), waits=[ty])
                        war[(c, sl, r)] = tok
                prevs[c] = lns[c](xrows, yv, prev=prevs[c],
                                  y_done=ydone, y_war=ywar)
            w = [(pfx + "s%d%d%d" % (c, sl, r),
                  p.counts.get(pfx + "s%d%d%d" % (c, sl, r), 0))
                 for c in (0, 1) for sl in (0, 1) for r in (0, 1)]
            coll_fn(b, [t for t in w if t[1]])
    return


def emit_x2(nc, p, x1_ap, a2a_h_out, x2_ap, nb=B_PER, pfx="x2"):
    with ExitStack() as ctx:
        hb = [sbt(nc, ctx, f"x2h{i}", [128, SJ], BF16) for i in range(2)]
        xb = [sbt(nc, ctx, f"x2x{i}", [128, SJ], F32) for i in range(2)]
        ob = [sbt(nc, ctx, f"x2o{i}", [128, SJ], F32) for i in range(2)]
        st = [None, None]
        ta_prev = [None, None]
        ld = [None, None, None, None]
        n = 0
        for b in range(nb):
            for ec in range(NDC):
                for j in range(NJ):
                    sl = n % 2
                    n += 1
                    th = []
                    for hh in range(2):
                        row0 = ((2 * j + hh) * nb + b) * D + ec * 128
                        tt = p.dma("gpsimd", hb[sl][:, hh * POS:(hh + 1) * POS],
                                   a2a_h_out.ap()[row0:row0 + 128, :],
                                   sem=pfx + "h%d%d" % (sl, hh),
                                   waits=[ta_prev[sl]])
                        th.append(tt)
                    tx = p.dma("sync", xb[sl][:],
                               x1_ap[b, ec * 128:(ec + 1) * 128,
                                     j * SJ:(j + 1) * SJ],
                               sem=pfx + "x%d" % sl, waits=[ta_prev[sl]])
                    ta = p.dve("tensor_add", ob[sl][:], xb[sl][:], hb[sl][:],
                               waits=th + [tx, st[sl]])
                    ta_prev[sl] = ta
                    st[sl] = p.dma("gpsimd",
                                   x2_ap[b, ec * 128:(ec + 1) * 128,
                                         j * SJ:(j + 1) * SJ],
                                   ob[sl][:], sem=pfx + "s%d" % sl, waits=[ta])
    return st


def build_fused(nb=B_PER):
    nc = bass.Bass("TRN2", target_bir_lowering=False)
    ein = lambda nm, shp: nc.dram_tensor(nm, shp, F32, kind="ExternalInput").ap()
    drA = {"x": ein("x", [nb, D, S]),
           "wqkvT": ein("wqkvT", [D, 3 * D]),
           "woutT": ein("woutT", [D, D]),
           "bout_rs": ein("bout_rs", [128, NDC]),
           "cos_rep": ein("cos_rep", [128, S]),
           "sin_rep": ein("sin_rep", [128, S])}
    wg_d = {nm: ein(d, [D, D]) for nm, d in
            (("wf", "wfT"), ("uf", "ufT"), ("wi", "wiT"), ("ui", "uiT"),
             ("wc", "wcT"), ("uc", "ucT"), ("wo", "woT"), ("uo", "uoT"))}
    bias_d = {nm: ein(d, [128, NDC]) for nm, d in
              (("bf", "bf_rs"), ("bi", "bi_rs"), ("bc", "bc_rs"),
               ("bo", "bo_rs"), ("inc", "inc_rs"))}
    drA1 = {"Wq": ein("Wq", [D, D]), "Wk": ein("Wk", [D, D]),
            "Wv": ein("Wv", [D, D]), "woutT": ein("Wo1", [D, D]),
            "bout_rs": ein("bo1_rs", [128, NDC])}
    drF1 = {"WrT": ein("WrT", [D, D]), "WkT": ein("WkT", [D, D]),
            "WvT": ein("WvT", [D, D]), "mur_rs": ein("mur_rs", [128, NDC]),
            "muk_rs": ein("muk_rs", [128, NDC])}
    x1_t = nc.dram_tensor("x1", [nb, D, S], F32, kind="Internal")
    x2_t = nc.dram_tensor("x2", [nb, D, S], F32, kind="Internal")
    x3_t = nc.dram_tensor("x3", [nb, D, S], F32, kind="Internal")
    a2a_y_in = [nc.dram_tensor("a2a_y_in%d" % k, [NCORES * D, POS], BF16,
                               kind="Internal") for k in range(nb)]
    a2a_y_out = [nc.dram_tensor("a2a_y_out%d" % k, [NCORES * D, POS], BF16,
                                kind="Internal") for k in range(nb)]
    a2a_h_in = [nc.dram_tensor("a2a_h_in%d" % k, [NCORES * D, POS], BF16,
                               kind="Internal") for k in range(B_PER)]
    a2a_h_out = [nc.dram_tensor("a2a_h_out%d" % k, [NCORES * D, POS], BF16,
                                kind="Internal") for k in range(B_PER)]
    x4_d = nc.dram_tensor("x4", [nb, D, S], F32, kind="ExternalOutput").ap()

    drA["x_out"] = x1_t.ap()
    drA1["x"] = x2_t.ap()
    drA1["x_out"] = x3_t.ap()
    drF1["x"] = x3_t.ap()
    drF1["x_out"] = x4_d

    with ExitStack() as octx:
        psX = [pst(nc, octx, f"ps{i}", [128, 1024]) for i in range(4)]
        p = Plan(nc, octx)

        def y_coll(k, waits):
            p.emit("gpsimd",
                   lambda e, k=k: e.collective_compute(
                       "AllToAll", AL.bypass,
                       replica_groups=[list(range(NCORES))],
                       ins=[a2a_y_in[k].ap().opt()],
                       outs=[a2a_y_out[k].ap().opt()]),
                   waits=waits, inc="ccy", inc_n=1)
        y_stage = lambda b, j, dc, r: \
            a2a_y_in[b].ap()[(2 * j + r) * D + dc * 128:
                             (2 * j + r) * D + dc * 128 + 128, :]
        emit_attn(nc, p, psX, drA, rope=True, nb=nb, pfx="a0")
        p.barrier()
        y_coll_args = []
        emit_ln_y1(nc, p, psX, x1_t.ap(), y_stage,
                   lambda k, w: y_coll_args.append((k, w)), nb)
        p.barrier(exclude=("ccy",))
        def ysrc(t):
            return a2a_y_out[t // NCORES].ap()[
                (t % NCORES) * D:(t % NCORES + 1) * D, :].rearrange(
                "(a b) c -> b a c", b=128)
        def hdst(t, mc):
            row = (t % NCORES) * D + mc * 128
            return a2a_h_in[t // NCORES].ap()[row:row + 128, :]

        def h_coll(k, waits):
            p.emit("gpsimd",
                   lambda e, k=k: e.collective_compute(
                       "AllToAll", AL.bypass,
                       replica_groups=[list(range(NCORES))],
                       ins=[a2a_h_in[k].ap().opt()],
                       outs=[a2a_h_out[k].ap().opt()]),
                   waits=waits, inc="cch", inc_n=1)
        x2sp = {"x1_ap": x1_t.ap(), "h_out": lambda k: a2a_h_out[k],
                "x2_ap": x2_t.ap(), "gate": lambda k: [("cch", k + 1)]}
        drF0 = {"wg": wg_d, "bias": bias_d, "ysrc": ysrc, "hdst": hdst,
                "collh": h_coll, "x2sp": x2sp,
                "pre_colls": lambda: [y_coll(k, w) for k, w in y_coll_args]}
        emit_ffn0_core(nc, p, psX, drF0, y_ready=[], pfx="f0",
                       ygate=lambda t: [("ccy", t // NCORES + 1)])
        p.barrier()
        emit_attn(nc, p, psX, drA1, rope=False, nb=nb, pfx="a1")
        p.barrier()
        emit_ffn1(nc, p, psX, drF1, nb=nb, pfx="f1")
        from concourse.aot_env import allow_ici_all_to_all
        with allow_ici_all_to_all():
            finish(nc, p)
    return nc


def prep_fused(inputs):
    m = {}
    m.update(prep_l0_attn(inputs))
    m.update(prep_l0_ffn(inputs))
    m.update(prep_l1_ffn(inputs))
    m["Wq"] = inputs["L1_attn_Wq"]
    m["Wk"] = inputs["L1_attn_Wk"]
    m["Wv"] = inputs["L1_attn_Wv"]
    m["Wo1"] = inputs["L1_attn_Wo"]
    m["bo1_rs"] = _rs(inputs["L1_attn_bo"])
    return m


def _perm_qk():
    perm = np.zeros(D, dtype=np.int64)
    for n in range(512):
        perm[n] = 64 * (n // 32) + 2 * (n % 32)
        perm[512 + n] = 64 * (n // 32) + 2 * (n % 32) + 1
    return perm


def _rope_tables_rep():
    inv = (np.float32(1.0) / (np.float32(10000.0) **
           (np.arange(0, HD, 2, dtype=np.float32) / np.float32(HD)))).astype(np.float32)
    t = np.arange(S, dtype=np.float32)
    fr = np.outer(t, inv).astype(np.float32)
    cos = np.cos(fr).astype(np.float32).T
    sin = np.sin(fr).astype(np.float32).T
    return (np.ascontiguousarray(np.tile(cos, (4, 1))),
            np.ascontiguousarray(np.tile(sin, (4, 1))))


def _rs(v):
    return np.ascontiguousarray(v.reshape(NDC, 128).T)


def prep_l0_attn(inputs):
    wT = np.ascontiguousarray(inputs["L0_attn_w_qkv"].T)
    perm = _perm_qk()
    wqkvT = np.ascontiguousarray(np.concatenate(
        [wT[:, 0:D][:, perm], wT[:, D:2 * D][:, perm], wT[:, 2 * D:3 * D]],
        axis=1))
    cos_rep, sin_rep = _rope_tables_rep()
    return {"wqkvT": wqkvT,
            "woutT": np.ascontiguousarray(inputs["L0_attn_w_out"].T),
            "bout_rs": _rs(inputs["L0_attn_b_out"]),
            "cos_rep": cos_rep, "sin_rep": sin_rep}


def prep_l1_attn(inputs):
    return {"Wq": inputs["L1_attn_Wq"], "Wk": inputs["L1_attn_Wk"],
            "Wv": inputs["L1_attn_Wv"], "woutT": inputs["L1_attn_Wo"],
            "bout_rs": _rs(inputs["L1_attn_bo"])}


def run_attn(inputs, Xf, rope):
    nc = build_attn_layer(rope=rope)
    common = prep_l0_attn(inputs) if rope else prep_l1_attn(inputs)
    in_maps = []
    for i in range(NCORES):
        m = dict(common)
        m["x"] = np.ascontiguousarray(Xf[B_PER * i:B_PER * (i + 1)])
        in_maps.append(m)
    res = run_bass_kernel_spmd(nc, in_maps, core_ids=list(range(NCORES)))
    return np.concatenate([r["x1"] for r in res.results], axis=0)


def kernel(**inputs):
    Xf = np.ascontiguousarray(inputs["L0_attn_x"].transpose(0, 2, 1))
    nc = build_fused()
    common = prep_fused(inputs)
    in_maps = []
    for i in range(NCORES):
        m = dict(common)
        m["x"] = np.ascontiguousarray(Xf[i::NCORES])
        in_maps.append(m)
    res = run_bass_kernel_spmd(nc, in_maps, core_ids=list(range(NCORES)))
    X4f = np.empty((NCORES * B_PER, D, S), np.float32)
    for i in range(NCORES):
        X4f[i::NCORES] = res.results[i]["x4"]
    return np.ascontiguousarray(X4f.transpose(0, 2, 1))


def kernel_unfused(**inputs):
    Xf = np.ascontiguousarray(inputs["L0_attn_x"].transpose(0, 2, 1))
    X1f = run_attn(inputs, Xf, rope=True)
    X2f = run_l0_ffn(inputs, X1f)
    X3f = run_attn(inputs, X2f, rope=False)
    X4f = run_l1_ffn(inputs, X3f)
    return np.ascontiguousarray(X4f.transpose(0, 2, 1))


# version 18
# speedup vs baseline: 1.0168x; 1.0168x over previous
import numpy as np
import ml_dtypes
from contextlib import ExitStack

import concourse.bass as bass
import concourse.mybir as mybir
from concourse.bass_utils import run_bass_kernel_spmd

F32 = mybir.dt.float32
BF16 = mybir.dt.bfloat16
AF = mybir.ActivationFunctionType
AL = mybir.AluOpType
AP = bass.AP

NCORES = 8
B_PER = 4
S = 2048
D = 1024
NDC = D // 128
H = 16
HD = 64
SJ = 512
NJ = S // SJ
NKT = S // 128
POS = S // NCORES
T = 32

LN_EPS = 1e-5
EXP_BIAS = -4.0

ENGINES = ("sync", "gpsimd", "tensor", "vector", "scalar")


class Plan:
    def __init__(self, nc, ctx):
        self.nc = nc
        self.ctx = ctx
        self.ops = {e: [] for e in ENGINES}
        self.sems = {}
        self.counts = {}
        self.pending = {e: [] for e in ENGINES}
        self.s7 = None
        for name in ("pe", "act", "dve", "pl", "cc"):
            self.get_sem(name)

    def barrier(self, exclude=()):
        snap = [(n, c) for n, c in self.counts.items()
                if c and n not in exclude]
        for e in ENGINES:
            self.pending[e] = list(snap)

    def get_sem(self, name):
        if name not in self.sems:
            self.sems[name] = self.ctx.enter_context(self.nc.semaphore(name))
            self.counts[name] = 0
        return self.sems[name]

    def emit(self, eng, fn, waits=(), inc=None, inc_n=1):
        waits = tuple(self.pending[eng] + [w for w in waits if w is not None])
        if self.pending[eng]:
            self.pending[eng] = []
        if inc is not None:
            self.get_sem(inc)
        incs = []
        if inc is not None:
            self.counts[inc] += inc_n
            incs.append((inc, self.counts[inc], inc_n))
        self.ops[eng].append((waits, fn, tuple(incs)))
        return (inc, self.counts[inc]) if inc else None

    def dma(self, eng, out, in_, sem, waits=(), **kw):
        self.get_sem(sem)
        return self.emit(eng, lambda e: e.dma_start(out=out, in_=in_, **kw),
                         waits, inc=sem, inc_n=16)

    def mm(self, out, lhsT, rhs, start, stop, waits=(), tp=None, inc=False,
           sgc=False):
        kw = {}
        if tp is not None:
            kw["tile_position"] = tp
        if sgc:
            kw["skip_group_check"] = True
        return self.emit(
            "tensor",
            lambda e: e.matmul(out, lhsT, rhs, start=start, stop=stop, **kw),
            waits, inc=("pe" if inc else None))

    def act(self, out, in_, func, waits=(), bias=0.0, scale=1.0, accum=None):
        kw = {"bias": bias, "scale": scale}
        if accum is not None:
            kw["accum_out"] = accum
        return self.emit("scalar",
                         lambda e: e.activation(out, in_, func, **kw),
                         waits, inc="act")

    def dve(self, fname, *args, waits=(), **kw):
        return self.emit("vector", lambda e: getattr(e, fname)(*args, **kw),
                         waits, inc="dve")

    def pool(self, fname, *args, waits=(), **kw):
        waits = list(waits) + [("pl", self.counts["pl"])]
        return self.emit("gpsimd", lambda e: getattr(e, fname)(*args, **kw),
                         waits, inc="pl", inc_n=1)

    def replay(self, block):
        own = {"vector": "dve", "scalar": "act"}
        def runner(engname):
            def f(eng):
                last = {}
                i = 0
                osem = own.get(engname)
                for waits, fn, incs in self.ops[engname]:
                    if osem is not None and i > 0:
                        ws = list(waits) + [(osem, i)]
                    else:
                        ws = waits
                    for s, v in ws:
                        if last.get(s, 0) < v:
                            eng.wait_ge(self.sems[s], v)
                            last[s] = v
                    inst = fn(eng)
                    for s, v, n in incs:
                        inst.then_inc(self.sems[s], n)
                    i += 1
            return f
        block.sync(runner("sync"))
        block.gpsimd(runner("gpsimd"))
        block.tensor(runner("tensor"))
        block.vector(runner("vector"))
        block.scalar(runner("scalar"))


_SBT_N = [0]


def sbt(nc, ctx, name, shape, dt):
    _SBT_N[0] += 1
    return ctx.enter_context(nc.sbuf_tensor("%s_%d" % (name, _SBT_N[0]),
                                            shape, dt))


def pst(nc, ctx, name, shape, dt=F32):
    return ctx.enter_context(nc.psum_tensor(name, shape, dt))


def make_ln(nc, ctx, p, ntok, ps_stat, ps_bc, pfx="", use_s7=True,
            dma_q="gpsimd"):
    xbf = sbt(nc, ctx, "ln_xbf", [128, 2 * ntok], BF16)
    xsq = sbt(nc, ctx, "ln_xsq", [128, ntok], BF16)
    rows = sbt(nc, ctx, "ln_rows", [128, 2 * ntok], F32)
    rows_bf = sbt(nc, ctx, "ln_rows_bf", [1, 2 * ntok], BF16)
    mrbc = sbt(nc, ctx, "ln_mrbc", [128, ntok], BF16)
    ones_bf = sbt(nc, ctx, "ln_ones_bf", [128, 1], BF16)
    ones1f = sbt(nc, ctx, "ln_ones1f", [1, 128], BF16)
    eps_sb = sbt(nc, ctx, "ln_eps", [1, 1], F32)
    p.pool("memset", ones_bf[:], 1.0)
    p.pool("memset", ones1f[:], 1.0)
    p.pool("memset", eps_sb[:], LN_EPS)

    def ln(xrows, y_view, wait_loads=(), prev=None, y_done=None, y_war=None):
        prev = prev or {}
        t_x = []
        stat_last = None
        ld_prev = [prev.get("ld0"), prev.get("ld1")]
        sq_prev = prev.get("sq")
        rd_prev = [None, None]
        mm_prev = [None, None]
        for dc in range(NDC):
            o = (dc % 2) * ntok
            tb = p.dma(dma_q, xbf[:, o:o + ntok], xrows(dc),
                       sem=pfx + "lnx%d" % (dc % 2),
                       waits=list(wait_loads) + [ld_prev[dc % 2],
                                                 rd_prev[dc % 2],
                                                 mm_prev[dc % 2],
                                                 prev.get("apply")])
            ld_prev[dc % 2] = tb
            t_x.append(tb)
            p.mm(ps_stat[0:1, 0:ntok], ones_bf[:, 0:1],
                 xbf[:, o:o + ntok],
                 start=(dc == 0), stop=(dc == NDC - 1),
                 waits=[tb] + ([prev.get("rows_dve"),
                                p.s7 if use_s7 else prev.get("apply")]
                               if dc == 0 else []),
                 sgc=True)
            ts_ = p.act(xsq[:], xbf[:, o:o + ntok], AF.Square,
                        waits=[tb, sq_prev])
            rd_prev[dc % 2] = ts_
            stat_last = p.mm(ps_stat[32:33, 0:ntok], ones_bf[:, 0:1], xsq[:],
                             start=(dc == 0), stop=(dc == NDC - 1),
                             waits=[ts_], inc=True, sgc=True)
            mm_prev[dc % 2] = stat_last
            sq_prev = stat_last
        m_row = rows[64:65, 0:ntok]
        m2_row = rows[64:65, ntok:2 * ntok]
        var_row = rows[0:1, ntok:2 * ntok]
        r_row = rows[0:1, 0:ntok]
        p.dve("tensor_scalar_mul", m_row, ps_stat[0:1, 0:ntok], 1.0 / D,
              waits=[stat_last])
        p.dve("tensor_mul", m2_row, m_row, m_row)
        tv = p.dve("scalar_tensor_tensor", var_row, ps_stat[32:33, 0:ntok],
                   1.0 / D, m2_row, AL.mult, AL.subtract)
        tstd = p.act(var_row, var_row, AF.Sqrt, bias=eps_sb[:], waits=[tv])
        tr0 = p.dve("reciprocal", r_row, var_row, waits=[tstd])
        rbf = rows_bf[:, 0:ntok]
        mrbf = rows_bf[:, ntok:2 * ntok]
        tr = p.dve("tensor_copy", rbf, r_row)
        tmr = p.dve("scalar_tensor_tensor", mrbf, ps_stat[0:1, 0:ntok],
                    1.0 / D, r_row, AL.mult, AL.mult)
        tmb = p.mm(ps_bc[:, 0:ntok], ones1f[0:1, :], mrbf, start=True,
                   stop=True, waits=[tmr, tv, prev.get("mrbc_dve")], inc=True)
        tme = p.dve("tensor_copy", mrbc[:], ps_bc[:, 0:ntok], waits=[tmb])
        trb = p.mm(ps_bc[:, 0:ntok], ones1f[0:1, :], rbf, start=True,
                   stop=True, waits=[tr, tme, prev.get("apply")], inc=True)
        ty = None
        mul_prev = [None, None]
        for dc in range(NDC):
            o = (dc % 2) * ntok
            tb2 = p.dma(dma_q, xbf[:, o:o + ntok], xrows(dc),
                        sem=pfx + "lnx%d" % (dc % 2),
                        waits=[ld_prev[dc % 2], mul_prev[dc % 2],
                               stat_last if dc < 2 else None])
            ld_prev[dc % 2] = tb2
            tm = p.dve("tensor_mul", y_view(dc), xbf[:, o:o + ntok],
                       ps_bc[:, 0:ntok],
                       waits=[tb2, trb] + (y_war(dc) if y_war else []))
            mul_prev[dc % 2] = tm
            ty = p.dve("tensor_sub", y_view(dc), y_view(dc), mrbc[:])
            if y_done is not None:
                y_done(dc, ty)
        if use_s7:
            p.s7 = ty
        return {"apply": ty, "stats": stat_last, "rows_dve": tv,
                "mrbc_dve": tme, "x_loads": t_x, "ld0": ld_prev[0],
                "ld1": ld_prev[1], "sq": sq_prev}
    return ln


def emit_attn(nc, p, psX, dr, rope, nb=B_PER, pfx="", post=None):
    import collections
    x_d = dr["x"]
    x1_d = dr["x_out"]
    if rope:
        wqkv_d = dr["wqkvT"]
    else:
        wq_d, wk_d, wv_d = dr["Wq"], dr["Wk"], dr["Wv"]
    wout_d = dr["woutT"]
    bout_d = dr["bout_rs"]
    if rope:
        cos_d, sin_d = dr["cos_rep"], dr["sin_rep"]

    with ExitStack() as ctx:
        VST = 1536
        wbuf = sbt(nc, ctx, "wbuf", [128, 8 * D], BF16)
        y_sb = sbt(nc, ctx, "y_sb", [128, NDC * S], BF16)
        stage = sbt(nc, ctx, "stage", [128, 4096], BF16)
        q_sb = sbt(nc, ctx, "q_sb", [128, NDC * S], BF16)
        k_sb = sbt(nc, ctx, "k_sb", [128, NDC * S], BF16)
        v_sb = sbt(nc, ctx, "v_sb", [128, NKT * VST], BF16)
        av_sb = sbt(nc, ctx, "av_sb", [128, NDC * SJ], BF16)
        rs0 = sbt(nc, ctx, "rs0", [128, 512], BF16)
        rs1 = sbt(nc, ctx, "rs1", [128, 512], BF16)
        if rope:
            cos_sb = sbt(nc, ctx, "cos_sb", [128, S], BF16)
            sin_sb = sbt(nc, ctx, "sin_sb", [128, S], BF16)
        recip_sb = sbt(nc, ctx, "recip_sb", [1, SJ], BF16)
        rb_sb = sbt(nc, ctx, "rb_sb", [128, SJ], F32)
        rb_bf = sbt(nc, ctx, "rb_bf", [128, SJ], BF16)
        ones64f = sbt(nc, ctx, "ones64f", [1, 64], BF16)
        bout_sb = sbt(nc, ctx, "bout_sb", [128, NDC], F32)
        x1t = sbt(nc, ctx, "x1t", [128, SJ], F32)
        wo_ec = [sbt(nc, ctx, f"wo_ec{i}", [128, 1024], BF16) for i in range(2)]
        ebias_sb = sbt(nc, ctx, "ebias", [128, 1], F32)

        BG7 = psX[3][:, 512:1024]
        ln = make_ln(nc, ctx, p, SJ, ps_stat=psX[3][0:64, 512:1024],
                     ps_bc=BG7, pfx=pfx)
        if post is not None:
            y1buf = sbt(nc, ctx, "y1buf", [128, SJ], BF16)
            y1war = {}

            def y1_thunk(b, j, wl):
                def th(b=b, j=j, wl=wl):
                    def xrows(dc, b=b, j=j):
                        return post["x1_ap"][b, dc * 128:(dc + 1) * 128,
                                             j * SJ:(j + 1) * SJ]
                    def yv(dc):
                        return y1buf[:, :]
                    def ywar(dc):
                        return [y1war.get(0), y1war.get(1)]
                    def ydone(dc, ty, b=b, j=j):
                        for r in (0, 1):
                            tok = p.dma(
                                "gpsimd", post["stage"](b, j, dc, r),
                                y1buf[:, r * POS:(r + 1) * POS],
                                sem=pfx + "y1s%d" % r, waits=[ty])
                            y1war[r] = tok
                    st["ln_prev"] = ln(xrows, yv, wait_loads=wl,
                                       prev=st.get("ln_prev"),
                                       y_done=ydone, y_war=ywar)
                return th

            def y1_coll(k):
                w = [(pfx + "y1s%d" % r, p.counts.get(pfx + "y1s%d" % r, 0))
                     for r in (0, 1)]
                post["collective"](k, [t for t in w if t[1]])

        p.pool("memset", ones64f[:], 1.0)
        p.pool("memset", ebias_sb[:], EXP_BIAS)
        t_ones = None
        for pr in range(8):
            ones_ap = AP(v_sb, pr * 192 + 64,
                         [[NKT * VST, 128], [VST, NKT], [1, 64]])
            t_ones = p.pool("memset", ones_ap, 1.0)
        p.dma("gpsimd", bout_sb[:], bout_d[:, :], sem=pfx + "cst")
        if rope:
            p.dma("gpsimd", cos_sb[:], cos_d[:, :], sem=pfx + "cst")
            t_cst = p.dma("gpsimd", sin_sb[:], sin_d[:, :], sem=pfx + "cst")
        else:
            t_cst = (pfx + "cst", p.counts[pfx + "cst"])
        t_bout = t_cs = t_cst

        st = {"w_free": None,
              "tv": None,
              "kd": None,
              "qd0": None, "qd1": None,
              "rope_kd": None, "rope_qd0": None, "rope_qd1": None,
              "av_free": None,
              "y_rd": None,
              "wo_rd": [None, None],
              "x1_st0": None, "x1_ld0": None, "x1_st1": None,
              "x1_ld1": None, "norm_bc": None, "b5_rd": None}
        if not hasattr(p, "attn_hist"):
            p.attn_hist = {}
        hist = p.attn_hist.setdefault(pfx, {
            "exp_hist": {}, "av_rd": {}, "norm_free": [None, None]})
        exp_hist = hist["exp_hist"]
        av_rd = hist["av_rd"]
        norm_free = hist["norm_free"]

        bg = collections.deque()

        def emit_bg(n=1):
            for _ in range(n):
                if bg:
                    bg.popleft()()

        def drain_bg():
            while bg:
                bg.popleft()()

        def load_block(key):
            def th(key=key):
                tok = None
                for dc in range(NDC):
                    if rope and key < 3:
                        src = wqkv_d[dc * 128:(dc + 1) * 128,
                                     key * D:(key + 1) * D]
                    else:
                        src = {0: wq_d, 1: wk_d, 2: wv_d}[key]
                        src = src[dc * 128:(dc + 1) * 128, :]
                    tok = p.dma("gpsimd", wbuf[:, dc * D:(dc + 1) * D], src,
                                sem=pfx + "wld", waits=[st["w_free"]])
                st["wld"] = tok
            return th

        pro_slots = [(psX[0], 0), (psX[0], 512), (psX[1], 0), (psX[1], 512)]

        def pro_slot():
            q = st["pro_n"] % 4
            st["pro_n"] += 1
            tens, off = pro_slots[q]
            w = [st["pro_evac"][q]]
            if st["pro_fresh"][q]:
                w.append(exp_hist.get((q // 2, off // 512)))
                st["pro_fresh"][q] = False
            return tens[:, off:off + 512], (tens, off), w, q

        def v_group(sc, dv2):
            def th(sc=sc, dv2=dv2):
                ps, (tens, off), wslot, q = pro_slot()
                for dc in range(NDC):
                    w0 = ([st["wld"], st["y_tok"]] + wslot) if dc == 0 else []
                    t = p.mm(ps,
                             y_sb[:, dc * S + sc * 128:dc * S + (sc + 1) * 128],
                             wbuf[:, dc * D + dv2 * 512:dc * D + (dv2 + 1) * 512],
                             start=(dc == 0), stop=(dc == NDC - 1),
                             waits=w0, inc=(dc == NDC - 1))
                    if dc == NDC - 1:
                        st["w_free"] = t
                dst0 = sc * VST + 4 * dv2 * 192
                p.dve("tensor_copy",
                      AP(v_sb, dst0, [[NKT * VST, 128], [192, 4], [1, 64]]),
                      AP(tens, off, [[1024, 128], [128, 4], [1, 64]]),
                      waits=[t, t_ones, hist.get("avv")])
                tok = p.dve(
                    "tensor_copy",
                    AP(v_sb, dst0 + 128, [[NKT * VST, 128], [192, 4], [1, 64]]),
                    AP(tens, off + 64, [[1024, 128], [128, 4], [1, 64]]))
                st["tv"] = tok
                st["pro_evac"][q] = tok
            return th

        def qk_group(dst_sb, ec, j, key, bg7=False):
            def th(dst_sb=dst_sb, ec=ec, j=j, key=key, bg7=bg7):
                if bg7:
                    ps, wslot = BG7[:, 0:512], [p.s7]
                else:
                    ps, _, wslot, q = pro_slot()
                for dc in range(NDC):
                    w0 = ([st["wld"], st["y_tok"]] + wslot) if dc == 0 else []
                    t = p.mm(ps,
                             wbuf[:, dc * D + ec * 128:dc * D + (ec + 1) * 128],
                             y_sb[:, dc * S + j * SJ:dc * S + (j + 1) * SJ],
                             start=(dc == 0), stop=(dc == NDC - 1),
                             waits=w0, inc=(dc == NDC - 1))
                    if dc == NDC - 1:
                        st["w_free"] = t
                        st["y_rd"] = t
                tok = p.dve("tensor_copy",
                            dst_sb[:, ec * S + j * SJ:ec * S + (j + 1) * SJ],
                            ps, waits=[t])
                st[key] = tok
                if bg7:
                    p.s7 = tok
                else:
                    st["pro_evac"][q] = tok
            return th

        def rope_cols(src_sb, key, quarters, use_dve=False):
            def th(src_sb=src_sb, key=key, quarters=quarters):
                last = None
                last2 = None
                ds0 = stage[:, 0:512]
                ds1 = stage[:, 512:1024]
                for c in range(4):
                    for qq in quarters:
                        o = qq * 512
                        QE = src_sb[:, c * S + o:c * S + o + 512]
                        QO = src_sb[:, (c + 4) * S + o:(c + 4) * S + o + 512]
                        cs = cos_sb[:, o:o + 512]
                        sn = sin_sb[:, o:o + 512]
                        if c % 2 == 0 or not use_dve:
                            p.pool("tensor_mul", rs0[:], QE, cs,
                                   waits=[st[key], t_cs])
                            p.pool("tensor_mul", rs1[:], QE, sn)
                            p.pool("tensor_mul", QE, QO, sn)
                            p.pool("tensor_sub", QE, rs0[:], QE)
                            p.pool("tensor_mul", QO, QO, cs)
                            last = p.pool("tensor_add", QO, rs1[:], QO)
                        else:
                            p.dve("tensor_mul", ds0[:], QE, cs,
                                  waits=[st[key], t_cs,
                                         av_rd.get((0, 0)),
                                         av_rd.get((1, 0))])
                            p.dve("tensor_mul", ds1[:], QE, sn)
                            p.dve("tensor_mul", QE, QO, sn)
                            p.dve("tensor_sub", QE, ds0[:], QE)
                            p.dve("tensor_mul", QO, QO, cs)
                            last2 = p.dve("tensor_add", QO, ds1[:], QO)
                st["rope_" + key] = last
                st["rope2_" + key] = last2
            return th

        def ln_thunks(b, wait_y):
            out = []
            for j in range(NJ):
                def th(b=b, j=j):
                    def xrows(dc, b=b, j=j):
                        return x_d[b, dc * 128:(dc + 1) * 128,
                                   j * SJ:(j + 1) * SJ]
                    def yv(dc, j=j):
                        return y_sb[:, dc * S + j * SJ:dc * S + (j + 1) * SJ]
                    st["ln_prev"] = ln(xrows, yv,
                                       wait_loads=[wait_y()],
                                       prev=st.get("ln_prev"))
                    st["y_tok"] = st["ln_prev"]["apply"]
                out.append(th)
            return out

        def vaug(sc, h):
            off = sc * VST + (h // 2) * 192 + (64 if h % 2 else 0)
            return AP(v_sb, off, [[NKT * VST, 128], [1, 128]])

        avps = [psX[2][:, 0:512], psX[3][:, 0:512]]

        for b in range(nb):
            st["pro_n"] = 0
            st["pro_evac"] = [None] * 4
            st["pro_fresh"] = [True] * 4
            if b == 0:
                for th in ln_thunks(0, lambda: None):
                    th()
            load_block(2)()
            for sc in range(NKT):
                for dv2 in range(2):
                    v_group(sc, dv2)()
            load_block(1)()
            for j in range(NJ):
                for ec in range(NDC):
                    qk_group(k_sb, ec, j, "kd")()
                if rope:
                    rope_cols(k_sb, "kd", (j,), use_dve=True)()
            load_block(0)()
            for j in (0, 1):
                for ec in range(NDC):
                    qk_group(q_sb, ec, j, "qd0")()
                if rope:
                    rope_cols(q_sb, "qd0", (j,), use_dve=True)()
            if not rope:
                st["rope_kd"] = st["kd"]
                st["rope_qd0"] = st["qd0"]

            sc_tok = {}
            if post is not None and b > 0:
                y1_wl = [st["x1_st0"], st["x1_st1"]]
            for j in range(NJ):
                if post is not None and b > 0:
                    bg.append(y1_thunk(b - 1, j, y1_wl))
                if j == 0:
                    bg.append(load_block(0))
                    for jj in (2, 3):
                        for ec in range(NDC):
                            bg.append(qk_group(q_sb, ec, jj, "qd1", bg7=True))
                    if rope:
                        bg.append(rope_cols(q_sb, "qd1", (2, 3)))
                    else:
                        def setrd():
                            st["rope_qd1"] = st["qd1"]
                        bg.append(setrd)
                if j == 1 and b + 1 < nb:
                    for th in ln_thunks(b + 1, lambda: st["y_rd"]):
                        bg.append(th)
                if j == 2:
                    drain_bg()
                for g2 in range(8):
                    heads = [2 * g2, 2 * g2 + 1]
                    epair = {}
                    for kt in range(16):
                        half = kt % 2
                        for m in range(2):
                            h = heads[m]
                            w0 = [exp_hist.get((m, half))]
                            if g2 == 0 and kt < 2:
                                w0 += [st["rope_kd"],
                                       st["rope_qd%d" % (j // 2)],
                                       st.get("rope2_kd"),
                                       st.get("rope2_qd%d" % (j // 2))]
                                if b == 0 and j == 0:
                                    w0.append(st["tv"])
                            ps = psX[m][:, half * 512:half * 512 + 512]
                            if rope:
                                strip = 32 * (h % 4)
                                ch = h // 4
                                p.mm(ps, k_sb[strip:strip + 32,
                                              ch * S + kt * 128:ch * S + (kt + 1) * 128],
                                     q_sb[strip:strip + 32,
                                          ch * S + j * SJ:ch * S + (j + 1) * SJ],
                                     start=True, stop=False, tp=(strip, 0),
                                     waits=w0, sgc=True)
                                t = p.mm(ps, k_sb[strip:strip + 32,
                                                  (ch + 4) * S + kt * 128:(ch + 4) * S + (kt + 1) * 128],
                                         q_sb[strip:strip + 32,
                                              (ch + 4) * S + j * SJ:(ch + 4) * S + (j + 1) * SJ],
                                         start=False, stop=True, tp=(strip, 0),
                                         inc=True, sgc=True)
                            else:
                                strip = 64 * (h % 2)
                                ch = h // 2
                                t = p.mm(ps, k_sb[strip:strip + 64,
                                                  ch * S + kt * 128:ch * S + (kt + 1) * 128],
                                         q_sb[strip:strip + 64,
                                              ch * S + j * SJ:ch * S + (j + 1) * SJ],
                                         start=True, stop=True, tp=(strip, 0),
                                         waits=w0, inc=True, sgc=True)
                            sc_tok[(m, half)] = t
                        for m in range(2):
                            slot = ((kt // 2 % 2) * 2 + m) * 1024 + half * 512
                            e = p.act(stage[:, slot:slot + 512],
                                      psX[m][:, half * 512:half * 512 + 512],
                                      AF.Exp, bias=ebias_sb[:], scale=0.125,
                                      waits=[sc_tok[(m, half)],
                                             av_rd.get((kt % 4, m))])
                            exp_hist[(m, half)] = e
                            epair[(m, kt)] = e
                        if kt >= 2:
                            ka = kt - 2
                            for m in range(2):
                                slot = ((ka // 2 % 2) * 2 + m) * 1024 + (ka % 2) * 512
                                w0 = [epair.pop((m, ka))]
                                if ka == 0:
                                    w0.append(norm_free[m])
                                tav = p.mm(avps[m], vaug(ka, heads[m]),
                                           stage[:, slot:slot + 512],
                                           start=(ka == 0), stop=False,
                                           waits=w0, inc=True, sgc=True)
                                av_rd[(ka % 4, m)] = tav
                        if half == 0:
                            emit_bg(1)
                    for ka in (14, 15):
                        for m in range(2):
                            slot = ((ka // 2 % 2) * 2 + m) * 1024 + (ka % 2) * 512
                            tav = p.mm(avps[m], vaug(ka, heads[m]),
                                       stage[:, slot:slot + 512],
                                       start=False, stop=(ka == 15),
                                       waits=[epair.pop((m, ka))], inc=True,
                                       sgc=True)
                            av_rd[(ka % 4, m)] = tav
                    t2s = []
                    for m in range(2):
                        h = heads[m]
                        dn_r = 64 if h % 2 == 0 else 0
                        t1 = p.dve("tensor_copy", recip_sb[:],
                                   avps[m][dn_r:dn_r + 1, 0:512],
                                   waits=[av_rd[(3, m)], exp_hist[(m, 1)]] + t2s)
                        t2 = p.mm(psX[2][64 * m:64 * m + 64, 512:1024],
                                  ones64f[0:1, :], recip_sb[:],
                                  start=True, stop=True,
                                  waits=[t1, st["norm_bc"], st["b5_rd"]],
                                  tp=(0, 64 * m) if m else None, inc=True)
                        t2s.append(t2)
                    tln = p.act(rb_sb[:], psX[2][:, 512:1024], AF.Ln,
                                waits=t2s)
                    st["norm_bc"] = tln
                    tex = p.act(rb_bf[:], rb_sb[:], AF.Exp, scale=-1.0,
                                waits=[tln])
                    for m in range(2):
                        h = heads[m]
                        av_r0 = 0 if h % 2 == 0 else 64
                        av_written = p.dve(
                            "tensor_mul",
                            av_sb[64 * (h % 2):64 * (h % 2) + 64,
                                  (h // 2) * SJ:(h // 2 + 1) * SJ],
                            avps[m][av_r0:av_r0 + 64, 0:512],
                            rb_bf[64 * m:64 * m + 64, :],
                            waits=[tex, st["av_free"]])
                        norm_free[m] = av_written
                hist["avv"] = av_rd[(3, 1)]
                op_banks = [psX[2][:, 512:1024], BG7]
                x1ts = [x1t[:], x1t[:]]
                for ec in range(NDC):
                    sl = ec % 2
                    bank = op_banks[sl]
                    tw = p.dma("gpsimd", wo_ec[sl][:],
                               wout_d[:, ec * 128:(ec + 1) * 128].rearrange(
                                   "(a q) c -> q a c", q=128),
                               sem=pfx + "wo%d" % sl, waits=[st["wo_rd"][sl]])
                    for dc in range(NDC):
                        if dc == 0:
                            w0 = [tw, av_written,
                                  (p.s7 if sl else st["norm_bc"]),
                                  (None if sl else st["b5_rd"])]
                        else:
                            w0 = []
                        t = p.mm(bank[:, 0:512],
                                 wo_ec[sl][:, dc * 128:(dc + 1) * 128],
                                 av_sb[:, dc * SJ:(dc + 1) * SJ],
                                 start=(dc == 0), stop=(dc == NDC - 1),
                                 waits=w0, inc=(dc == NDC - 1))
                    st["wo_rd"][sl] = t
                    st["av_free"] = t
                    tx = p.dma("sync", x1ts[sl],
                               x_d[b, ec * 128:(ec + 1) * 128,
                                   j * SJ:(j + 1) * SJ],
                               sem=pfx + "sx%d" % sl,
                               waits=[st["x1_st0"], st["x1_st1"],
                                      st["x1_ld0"], st["x1_ld1"]])
                    st["x1_ld%d" % sl] = tx
                    td = p.dve("scalar_tensor_tensor", x1ts[sl], bank[:, 0:512],
                               bout_sb[:, ec:ec + 1], x1ts[sl], AL.add, AL.add,
                               waits=[t, t_bout, tx])
                    if sl:
                        p.s7 = td
                    else:
                        st["b5_rd"] = td
                    st["x1_st%d" % sl] = p.dma(
                        "sync",
                        x1_d[b, ec * 128:(ec + 1) * 128, j * SJ:(j + 1) * SJ],
                        x1ts[sl], sem=pfx + "stx%d" % sl, waits=[td])
            drain_bg()
            if post is not None and b > 0:
                y1_coll(b - 1)
        if post is not None:
            wl = [st["x1_st0"], st["x1_st1"]]
            for j in range(NJ):
                y1_thunk(nb - 1, j, wl)()
            y1_coll(nb - 1)


def finish(nc, p):
    p.emit("sync", lambda e: e.nop(),
           waits=[(name, cnt) for name, cnt in p.counts.items() if cnt])
    with nc.Block() as block:
        p.replay(block)


def build_attn_layer(rope, nb=B_PER):
    nc = bass.Bass("TRN2", target_bir_lowering=False)
    dr = {"x": nc.dram_tensor("x", [nb, D, S], F32, kind="ExternalInput").ap()}
    if rope:
        dr["wqkvT"] = nc.dram_tensor("wqkvT", [D, 3 * D], F32, kind="ExternalInput").ap()
    else:
        for nm in ("Wq", "Wk", "Wv"):
            dr[nm] = nc.dram_tensor(nm, [D, D], F32, kind="ExternalInput").ap()
    dr["woutT"] = nc.dram_tensor("woutT", [D, D], F32, kind="ExternalInput").ap()
    dr["bout_rs"] = nc.dram_tensor("bout_rs", [128, NDC], F32, kind="ExternalInput").ap()
    if rope:
        dr["cos_rep"] = nc.dram_tensor("cos_rep", [128, S], F32, kind="ExternalInput").ap()
        dr["sin_rep"] = nc.dram_tensor("sin_rep", [128, S], F32, kind="ExternalInput").ap()
    dr["x_out"] = nc.dram_tensor("x1", [nb, D, S], F32, kind="ExternalOutput").ap()
    with ExitStack() as octx:
        psX = [pst(nc, octx, f"ps{i}", [128, 1024]) for i in range(4)]
        p = Plan(nc, octx)
        emit_attn(nc, p, psX, dr, rope, nb)
        finish(nc, p)
    return nc


def emit_ffn0_ln(nc, p, psX, x_d, y_hbm, nt=T, npos=POS, pfx="f0l"):
    NP8 = NDC * npos
    with ExitStack() as ctx:
        ybuf = [sbt(nc, ctx, f"ybuf{i}", [128, NP8], BF16) for i in range(2)]
        ln = make_ln(nc, ctx, p, npos, ps_stat=psX[3][0:64, 0:npos],
                     ps_bc=psX[3][:, 512:512 + npos], pfx=pfx)
        ln_prev = None
        y_str = [None, None]
        for t in range(nt):
            sl = t % 2
            def xrows(dc, t=t):
                return x_d[t, dc * 128:(dc + 1) * 128, :]
            def yv(dc, sl=sl):
                return ybuf[sl][:, dc * npos:(dc + 1) * npos]
            ln_prev = ln(xrows, yv, wait_loads=[y_str[sl]], prev=ln_prev)
            ydst = y_hbm.ap()[t].rearrange("(a b) c -> b a c", b=128)
            y_str[sl] = p.dma("gpsimd", ydst, ybuf[sl][:],
                              sem=pfx + "yst%d" % sl, waits=[ln_prev["apply"]])
    return [(pfx + "yst%d" % i, p.counts.get(pfx + "yst%d" % i, 0))
            for i in range(2)]


def emit_ffn0_core(nc, p, psX, dr, y_ready, nt=T, npos=POS, pfx="f0",
                   ygate=None):
    NP8 = NDC * npos
    wg_d, bias_d = dr["wg"], dr["bias"]
    ysrc = dr["ysrc"]
    with ExitStack() as ctx:
        wsb = {nm: sbt(nc, ctx, "w_" + nm, [128, 8 * D], BF16)
               for nm in ("wf", "wi", "wc", "uf", "ui", "uc")}
        c_sb = [sbt(nc, ctx, f"c_sb{i}", [128, NP8], F32) for i in range(2)]
        cbf = [sbt(nc, ctx, f"cbf{i}", [128, NP8], BF16) for i in range(2)]
        fga = {g: sbt(nc, ctx, f"g_{g}", [128, NP8], F32) for g in "fig"}
        tmp2 = [sbt(nc, ctx, f"scnt{i}", [128, npos], F32) for i in range(2)]
        cuo_sb = sbt(nc, ctx, "cuo_sb", [128, NP8], F32)
        ones_t = sbt(nc, ctx, "ones_t", [128, npos], F32)
        bsb = {nm: sbt(nc, ctx, "sb_" + nm, [128, NDC], F32)
               for nm in ("bf", "bi", "bc", "bo", "inc")}
        hbufs = [sbt(nc, ctx, f"hbuf{i}", [128, npos], F32) for i in range(2)]
        xfb = [sbt(nc, ctx, f"xfb{i}", [128, npos], F32) for i in range(2)]
        x2b = [sbt(nc, ctx, f"x2b{i}", [128, npos], F32) for i in range(2)]
        ylg = [sbt(nc, ctx, f"ylg{i}", [128, NP8], BF16) for i in range(2)]
        yl = [sbt(nc, ctx, f"yl{i}", [128, NP8], BF16) for i in range(2)]

        slots = [psX[0][:, 0:npos], psX[0][:, 512:512 + npos],
                 psX[1][:, 0:npos], psX[1][:, 512:512 + npos],
                 psX[2][:, 512:512 + npos]]
        NSL = len(slots)

        t_one = p.dve("memset", ones_t[:], 1.0)
        t_cst = None
        for nm in ("bf", "bi", "bc", "bo", "inc"):
            t_cst = p.dma("sync", bsb[nm][:], bias_d[nm][:, :],
                          sem=pfx + "cst")
        t_w = None
        for nm in ("wf", "wi", "wc", "uf", "ui", "uc"):
            for dc in range(NDC):
                t_w = p.dma("gpsimd", wsb[nm][:, dc * D:(dc + 1) * D],
                            wg_d[nm][dc * 128:(dc + 1) * 128, :],
                            sem=pfx + "wld")
        if dr.get("pre_colls"):
            dr["pre_colls"]()

        t_cbf = None
        for mc in range(NDC):
            p.dve("tensor_scalar_mul", c_sb[0][:, mc * npos:(mc + 1) * npos],
                  ones_t[:, :], bsb["inc"][:, mc:mc + 1],
                  waits=[t_cst, t_one])
            t_cbf = p.dve("tensor_copy", cbf[0][:, mc * npos:(mc + 1) * npos],
                          c_sb[0][:, mc * npos:(mc + 1) * npos])

        gate_bias = {"f": "bf", "i": "bi", "c": "bc"}
        gate_w = {"f": ("wf", "uf"), "i": ("wi", "ui"), "c": ("wc", "uc")}
        gate_buf = {"f": "f", "i": "i", "c": "g"}

        yg_free = [None, None]
        upd_tok = [None] * NDC
        slot_act = [None] * len(slots)
        nps = 0
        gmm_last = None
        tylg = [None, None]
        if ygate is None:
            ygate = lambda t: (list(y_ready) if t == 0 else [])
        tylg[0] = p.dma("sync", ylg[0][:], ysrc(0), sem=pfx + "yg0",
                        waits=list(y_ready) + ygate(0))
        for t in range(nt):
            sl = t % 2
            if t + 1 < nt:
                tylg[(t + 1) % 2] = p.dma(
                    "sync", ylg[(t + 1) % 2][:], ysrc(t + 1),
                    sem=pfx + "yg%d" % ((t + 1) % 2),
                    waits=[yg_free[(t + 1) % 2]] + ygate(t + 1))
            cp_ = t % 2
            for mc in range(NDC):
                sig_toks = {}
                for g in "fic":
                    wn, un = gate_w[g]
                    q = nps % NSL
                    ps = slots[q]
                    nps += 1
                    for dc in range(NDC):
                        w0 = []
                        if dc == 0:
                            w0 = [t_w, t_cbf, tylg[sl], slot_act[q]]
                        p.mm(ps, wsb[un][:, dc * D + mc * 128:dc * D + (mc + 1) * 128],
                             cbf[cp_][:, dc * npos:(dc + 1) * npos],
                             start=(dc == 0), stop=False, waits=w0, sgc=True)
                    for dc in range(NDC):
                        gmm_last = p.mm(
                            ps, wsb[wn][:, dc * D + mc * 128:dc * D + (mc + 1) * 128],
                            ylg[sl][:, dc * npos:(dc + 1) * npos],
                            start=False, stop=(dc == NDC - 1), sgc=True,
                            inc=(dc == NDC - 1))
                    fn_act = AF.Sigmoid if g in "fi" else AF.Tanh
                    sa = p.act(fga[gate_buf[g]][:, mc * npos:(mc + 1) * npos],
                               ps, fn_act, bias=bsb[gate_bias[g]][:, mc:mc + 1],
                               waits=[gmm_last, upd_tok[mc]])
                    slot_act[q] = sa
                    sig_toks[g] = sa
                cold = c_sb[cp_][:, mc * npos:(mc + 1) * npos]
                cnew = c_sb[1 - cp_][:, mc * npos:(mc + 1) * npos]
                p.dve("tensor_mul", tmp2[0][:], fga["f"][:, mc * npos:(mc + 1) * npos],
                      cold, waits=[sig_toks["f"]])
                p.dve("tensor_mul", tmp2[1][:], fga["i"][:, mc * npos:(mc + 1) * npos],
                      fga["g"][:, mc * npos:(mc + 1) * npos],
                      waits=[sig_toks["i"], sig_toks["c"]])
                p.dve("tensor_add", cnew, tmp2[0][:], tmp2[1][:])
                t_cbf = p.dve("tensor_copy",
                              cbf[1 - cp_][:, mc * npos:(mc + 1) * npos], cnew)
                upd_tok[mc] = t_cbf
            yg_free[sl] = gmm_last

        t_w2 = None
        for nm, key in (("wf", "wo"), ("wi", "uo")):
            for dc in range(NDC):
                t_w2 = p.dma("gpsimd", wsb[nm][:, dc * D:(dc + 1) * D],
                             wg_d[key][dc * 128:(dc + 1) * 128, :],
                             sem=pfx + "wld", waits=[gmm_last])
        for mc in range(NDC):
            q = nps % NSL
            ps = slots[q]
            nps += 1
            for dc in range(NDC):
                w0 = []
                if dc == 0:
                    w0 = [t_w2, t_cbf, slot_act[q]]
                tmm = p.mm(ps, wsb["wi"][:, dc * D + mc * 128:dc * D + (mc + 1) * 128],
                           cbf[nt % 2][:, dc * npos:(dc + 1) * npos],
                           start=(dc == 0), stop=(dc == NDC - 1), waits=w0,
                           inc=(dc == NDC - 1))
            te = p.dve("tensor_copy", cuo_sb[:, mc * npos:(mc + 1) * npos], ps,
                       waits=[tmm])
            slot_act[q] = te
        cuo_tok = te
        yl_free = [None, None]
        x2_st = [None, None]
        x2_ld = [None, None]
        hadd = [None, None]
        h_st = [None, None]
        resid = "x2" in dr
        collh = dr.get("collh")
        x2sp = dr.get("x2sp")
        if x2sp is not None:
            hbx = [sbt(nc, ctx, f"x2h{i}", [128, SJ], BF16) for i in range(2)]
            xbx = [sbt(nc, ctx, f"x2x{i}", [128, SJ], F32) for i in range(2)]
            obx = [sbt(nc, ctx, f"x2o{i}", [128, SJ], F32) for i in range(2)]
            xst = {"st": [None, None], "ta": [None, None], "n": 0}

            def emit_x2_part(k, part):
                gate = x2sp["gate"](k) if part == 0 else []
                h_ap = x2sp["h_out"](k).ap()
                for n4 in range(4):
                    ec, j = divmod(part * 4 + n4, NJ)
                    sl = xst["n"] % 2
                    xst["n"] += 1
                    th = []
                    for hh in (0, 1):
                        row0 = (2 * j + hh) * D + ec * 128
                        tt = p.dma(
                            "sync", hbx[sl][:, hh * POS:(hh + 1) * POS],
                            h_ap[row0:row0 + 128, :],
                            sem=pfx + "xh%d%d" % (sl, hh),
                            waits=[xst["ta"][sl]] + (gate if n4 == 0 else []))
                        th.append(tt)
                    tx = p.dma("sync", xbx[sl][:],
                               x2sp["x1_ap"][k, ec * 128:(ec + 1) * 128,
                                             j * SJ:(j + 1) * SJ],
                               sem=pfx + "xx%d" % sl,
                               waits=[xst["ta"][sl]])
                    ta = p.dve("tensor_add", obx[sl][:], xbx[sl][:],
                               hbx[sl][:], waits=th + [tx, xst["st"][sl]])
                    xst["ta"][sl] = ta
                    xst["st"][sl] = p.dma(
                        "sync",
                        x2sp["x2_ap"][k, ec * 128:(ec + 1) * 128,
                                      j * SJ:(j + 1) * SJ],
                        obx[sl][:], sem=pfx + "xs%d" % sl, waits=[ta])
            x2q = []
        for t in range(nt):
            sl = t % 2
            tyl = p.dma("sync", yl[sl][:], ysrc(t), sem=pfx + "ylb%d" % sl,
                        waits=[yl_free[sl]])
            for mc in range(NDC):
                q = nps % NSL
                ps = slots[q]
                nps += 1
                for dc in range(NDC):
                    w0 = []
                    if dc == 0:
                        w0 = [tyl, cuo_tok, slot_act[q]]
                    tmm = p.mm(ps, wsb["wf"][:, dc * D + mc * 128:dc * D + (mc + 1) * 128],
                               yl[sl][:, dc * npos:(dc + 1) * npos],
                               start=(dc == 0), stop=(dc == NDC - 1), waits=w0,
                               inc=(dc == NDC - 1))
                hsl = mc % 2
                tadd = p.dve("tensor_add", ps, ps,
                             cuo_sb[:, mc * npos:(mc + 1) * npos],
                             waits=[tmm, hadd[hsl] if resid else h_st[hsl]])
                if resid:
                    th = p.act(hbufs[hsl][:], ps, AF.Tanh,
                               bias=bsb["bo"][:, mc:mc + 1],
                               waits=[tadd, hadd[hsl]])
                    slot_act[q] = th
                    tx = p.dma("sync", xfb[hsl][:],
                               dr["x1s"][t, mc * 128:(mc + 1) * 128, :],
                               sem=pfx + "hx%d" % hsl,
                               waits=[x2_ld[hsl], hadd[hsl]])
                    x2_ld[hsl] = tx
                    ta = p.dve("tensor_add", x2b[hsl][:], xfb[hsl][:],
                               hbufs[hsl][:], waits=[th, tx, x2_st[hsl]])
                    hadd[hsl] = ta
                    x2_st[hsl] = p.dma("gpsimd",
                                       dr["x2"][t, mc * 128:(mc + 1) * 128, :],
                                       x2b[hsl][:], sem=pfx + "hs%d" % hsl,
                                       waits=[ta])
                else:
                    hb = cbf[0] if False else None
                    th = p.act(ylg[hsl][:, 0:npos], ps, AF.Tanh,
                               bias=bsb["bo"][:, mc:mc + 1],
                               waits=[tadd, h_st[hsl]])
                    slot_act[q] = th
                    h_st[hsl] = p.dma("sync", dr["hdst"](t, mc),
                                      ylg[hsl][:, 0:npos],
                                      sem=pfx + "hs%d" % hsl, waits=[th])
            yl_free[sl] = tmm
            if collh is not None and t % 8 == 7:
                collh(t // 8, [h_st[0], h_st[1]])
            if x2sp is not None:
                if t >= 10 and (t - 10) % 8 == 0 and (t - 10) // 8 < 3:
                    x2q.extend(((t - 10) // 8, pp) for pp in range(8))
                for _ in range(2):
                    if x2q:
                        emit_x2_part(*x2q.pop(0))
        if x2sp is not None:
            while x2q:
                emit_x2_part(*x2q.pop(0))
            for pp in range(8):
                emit_x2_part(nt // 8 - 1, pp)
    return [(pfx + "hs%d" % i, p.counts.get(pfx + "hs%d" % i, 0))
            for i in range(2)]


def build_l0_ffn(npos=POS, nt=T):
    nc = bass.Bass("TRN2", target_bir_lowering=False)
    x_d = nc.dram_tensor("x1s", [nt, D, npos], F32, kind="ExternalInput").ap()
    wg_d = {}
    for nm, d in (("wf", "wfT"), ("uf", "ufT"), ("wi", "wiT"), ("ui", "uiT"),
                  ("wc", "wcT"), ("uc", "ucT"), ("wo", "woT"), ("uo", "uoT")):
        wg_d[nm] = nc.dram_tensor(d, [D, D], F32, kind="ExternalInput").ap()
    bias_d = {}
    for nm, d in (("bf", "bf_rs"), ("bi", "bi_rs"), ("bc", "bc_rs"),
                  ("bo", "bo_rs"), ("inc", "inc_rs")):
        bias_d[nm] = nc.dram_tensor(d, [128, NDC], F32, kind="ExternalInput").ap()
    x2_d = nc.dram_tensor("x2", [nt, D, npos], F32, kind="ExternalOutput").ap()
    y_hbm = nc.dram_tensor("y2_hbm", [nt, D, npos], BF16, kind="Internal")
    with ExitStack() as octx:
        psX = [pst(nc, octx, f"ps{i}", [128, 1024]) for i in range(4)]
        p = Plan(nc, octx)
        y_ready = emit_ffn0_ln(nc, p, psX, x_d, y_hbm, nt, npos)
        def ysrc(t):
            return y_hbm.ap()[t].rearrange("(a b) c -> b a c", b=128)
        dr = {"wg": wg_d, "bias": bias_d, "ysrc": ysrc, "x1s": x_d, "x2": x2_d}
        emit_ffn0_core(nc, p, psX, dr, y_ready, nt, npos)
        finish(nc, p)
    return nc


def prep_l0_ffn(inputs):
    return {
        "wfT": np.ascontiguousarray(inputs["L0_ffn_wf"].T),
        "ufT": np.ascontiguousarray(inputs["L0_ffn_uf"].T),
        "wiT": np.ascontiguousarray(inputs["L0_ffn_wi"].T),
        "uiT": np.ascontiguousarray(inputs["L0_ffn_ui"].T),
        "wcT": np.ascontiguousarray(inputs["L0_ffn_wc"].T),
        "ucT": np.ascontiguousarray(inputs["L0_ffn_uc"].T),
        "woT": np.ascontiguousarray(inputs["L0_ffn_wo"].T),
        "uoT": np.ascontiguousarray(inputs["L0_ffn_uo"].T),
        "bf_rs": _rs(inputs["L0_ffn_bf"]), "bi_rs": _rs(inputs["L0_ffn_bi"]),
        "bc_rs": _rs(inputs["L0_ffn_bc"]), "bo_rs": _rs(inputs["L0_ffn_bo"]),
        "inc_rs": _rs(inputs["L0_ffn_in_c"]),
    }


def run_l0_ffn(inputs, X1f):
    nc = build_l0_ffn()
    common = prep_l0_ffn(inputs)
    in_maps = []
    for i in range(NCORES):
        m = dict(common)
        m["x1s"] = np.ascontiguousarray(X1f[:, :, POS * i:POS * (i + 1)])
        in_maps.append(m)
    res = run_bass_kernel_spmd(nc, in_maps, core_ids=list(range(NCORES)))
    out = np.empty((T, D, S), np.float32)
    for i in range(NCORES):
        out[:, :, POS * i:POS * (i + 1)] = res.results[i]["x2"]
    return out


def emit_ffn1(nc, p, psX, dr, nb=B_PER, pfx="f1"):
    x_d = dr["x"]
    wr_d, wk_d, wv_d = dr["WrT"], dr["WkT"], dr["WvT"]
    mur_d, muk_d = dr["mur_rs"], dr["muk_rs"]
    x4_d = dr["x_out"]

    with ExitStack() as ctx:
        wsb = {nm: sbt(nc, ctx, "w_" + nm, [128, 8 * D], BF16)
               for nm in ("wr", "wk", "wv")}
        y_sb = sbt(nc, ctx, "y_sb", [128, NDC * S], BF16)
        xr_sb = sbt(nc, ctx, "xr_sb", [128, NDC * SJ], BF16)
        xk_sb = sbt(nc, ctx, "xk_sb", [128, NDC * SJ], BF16)
        dsc = sbt(nc, ctx, "dsc", [128, SJ], BF16)
        r_sb = sbt(nc, ctx, "r_sb", [128, NDC * SJ], BF16)
        kact = sbt(nc, ctx, "kact", [128, NDC * SJ], BF16)
        krel = sbt(nc, ctx, "krel", [128, SJ], BF16)
        musb = {"r": sbt(nc, ctx, "mur_sb", [128, NDC], F32),
                "k": sbt(nc, ctx, "muk_sb", [128, NDC], F32)}
        tmpv = [sbt(nc, ctx, f"tmpv{i}", [128, SJ], F32) for i in range(2)]
        xfb = [sbt(nc, ctx, f"xfb{i}", [128, SJ], F32) for i in range(2)]
        x4b = [sbt(nc, ctx, f"x4b{i}", [128, SJ], F32) for i in range(2)]

        slots = [psX[0][:, 0:512], psX[0][:, 512:1024],
                 psX[1][:, 0:512], psX[1][:, 512:1024],
                 psX[2][:, 512:1024]]
        NSL = len(slots)
        ln = make_ln(nc, ctx, p, SJ, ps_stat=psX[3][0:64, 0:512],
                     ps_bc=psX[3][:, 512:1024], pfx=pfx)

        t_cst = p.dma("gpsimd", musb["r"][:], mur_d[:, :], sem=pfx + "cst")
        t_cst = p.dma("gpsimd", musb["k"][:], muk_d[:, :], sem=pfx + "cst")
        t_w = None
        for nm, d in (("wr", wr_d), ("wk", wk_d), ("wv", wv_d)):
            for dc in range(NDC):
                t_w = p.dma("gpsimd", wsb[nm][:, dc * D:(dc + 1) * D],
                            d[dc * 128:(dc + 1) * 128, :], sem=pfx + "wld")

        ln_prev = None
        slot_rd = [None] * NSL
        xmix_free = None
        r_free = None
        kact_free = None
        vres = [None, None]
        vres_act = [None, None]
        x4_st = [None, None]
        x4_ld = [None, None]
        y_stats_free = None
        for b in range(nb):
            for j in range(NJ):
                def xrows(dc, b=b, j=j):
                    return x_d[b, dc * 128:(dc + 1) * 128, j * SJ:(j + 1) * SJ]
                def yv(dc, b=b, j=j):
                    return y_sb[:, dc * S + j * SJ:dc * S + (j + 1) * SJ]
                ln_prev = ln(xrows, yv, wait_loads=[y_stats_free], prev=ln_prev)
            y_tok = ln_prev["apply"]
            for j in range(NJ):
                tmix = None
                for dc in range(NDC):
                    yj = y_sb[:, dc * S + j * SJ:dc * S + (j + 1) * SJ]
                    for side, dst in (("r", xr_sb), ("k", xk_sb)):
                        mu = musb[side][:, dc:dc + 1]
                        o = dc * SJ
                        if j == 0:
                            yp = y_sb[:, dc * S + j * SJ:dc * S + (j + 1) * SJ - 1]
                            p.dve("tensor_sub", dsc[:, 1:SJ],
                                  y_sb[:, dc * S + 1:dc * S + SJ], yp,
                                  waits=[y_tok, xmix_free])
                            p.dve("scalar_tensor_tensor", dst[:, o + 1:o + SJ],
                                  dsc[:, 1:SJ], mu, yp, AL.mult, AL.add,
                                  waits=[t_cst])
                            tmix = p.dve("tensor_scalar_mul", dst[:, o:o + 1],
                                         yj[:, 0:1], mu)
                        else:
                            yp = y_sb[:, dc * S + j * SJ - 1:dc * S + (j + 1) * SJ - 1]
                            p.dve("tensor_sub", dsc[:], yj, yp,
                                  waits=[y_tok, xmix_free])
                            tmix = p.dve("scalar_tensor_tensor", dst[:, o:o + SJ],
                                         dsc[:], mu, yp, AL.mult, AL.add,
                                         waits=[t_cst])
                nps = 0
                rk_tok = {}
                for side, src, dst in (("r", xr_sb, r_sb), ("k", xk_sb, kact)):
                    wn = "wr" if side == "r" else "wk"
                    for oc in range(NDC):
                        q = nps % NSL
                        ps = slots[q]
                        nps += 1
                        for dc in range(NDC):
                            w0 = []
                            if dc == 0:
                                w0 = [t_w, tmix, slot_rd[q]]
                            t = p.mm(ps, wsb[wn][:, dc * D + oc * 128:dc * D + (oc + 1) * 128],
                                     src[:, dc * SJ:(dc + 1) * SJ],
                                     start=(dc == 0), stop=(dc == NDC - 1),
                                     waits=w0, inc=(dc == NDC - 1))
                        if side == "r":
                            tk = p.act(dst[:, oc * SJ:(oc + 1) * SJ], ps,
                                       AF.Sigmoid, waits=[t, r_free])
                        else:
                            p.act(krel[:], ps, AF.Relu, waits=[t])
                            tk = p.act(dst[:, oc * SJ:(oc + 1) * SJ], krel[:],
                                       AF.Square, waits=[kact_free])
                        slot_rd[q] = tk
                        rk_tok[(side, oc)] = tk
                xmix_free = t
                for o2 in range(NDC):
                    q = nps % NSL
                    ps = slots[q]
                    nps += 1
                    for dc in range(NDC):
                        w0 = []
                        if dc == 0:
                            w0 = [rk_tok[("k", NDC - 1)], slot_rd[q]]
                        t = p.mm(ps, wsb["wv"][:, dc * D + o2 * 128:dc * D + (o2 + 1) * 128],
                                 kact[:, dc * SJ:(dc + 1) * SJ],
                                 start=(dc == 0), stop=(dc == NDC - 1),
                                 waits=w0, inc=(dc == NDC - 1))
                    kact_free = t
                    sl = o2 % 2
                    tv = p.dve("tensor_mul", tmpv[sl][:], ps,
                               r_sb[:, o2 * SJ:(o2 + 1) * SJ],
                               waits=[t, rk_tok[("r", o2)], vres[sl]])
                    slot_rd[q] = tv
                    r_free = tv
                    tx = p.dma("sync", xfb[sl][:],
                               x_d[b, o2 * 128:(o2 + 1) * 128, j * SJ:(j + 1) * SJ],
                               sem=pfx + "sx%d" % sl, waits=[x4_ld[sl], vres[sl]])
                    x4_ld[sl] = tx
                    ta = p.dve("tensor_add", x4b[sl][:], xfb[sl][:], tmpv[sl][:],
                               waits=[tx, x4_st[sl]])
                    vres[sl] = ta
                    x4_st[sl] = p.dma("gpsimd",
                                      x4_d[b, o2 * 128:(o2 + 1) * 128,
                                           j * SJ:(j + 1) * SJ],
                                      x4b[sl][:], sem=pfx + "stx%d" % sl, waits=[ta])
            y_stats_free = xmix_free


def build_l1_ffn(nb=B_PER):
    nc = bass.Bass("TRN2", target_bir_lowering=False)
    dr = {"x": nc.dram_tensor("x3", [nb, D, S], F32, kind="ExternalInput").ap(),
          "WrT": nc.dram_tensor("WrT", [D, D], F32, kind="ExternalInput").ap(),
          "WkT": nc.dram_tensor("WkT", [D, D], F32, kind="ExternalInput").ap(),
          "WvT": nc.dram_tensor("WvT", [D, D], F32, kind="ExternalInput").ap(),
          "mur_rs": nc.dram_tensor("mur_rs", [128, NDC], F32, kind="ExternalInput").ap(),
          "muk_rs": nc.dram_tensor("muk_rs", [128, NDC], F32, kind="ExternalInput").ap(),
          "x_out": nc.dram_tensor("x4", [nb, D, S], F32, kind="ExternalOutput").ap()}
    with ExitStack() as octx:
        psX = [pst(nc, octx, f"ps{i}", [128, 1024]) for i in range(4)]
        p = Plan(nc, octx)
        emit_ffn1(nc, p, psX, dr, nb)
        finish(nc, p)
    return nc


def prep_l1_ffn(inputs):
    return {"WrT": np.ascontiguousarray(inputs["L1_ffn_W_r"].T),
            "WkT": np.ascontiguousarray(inputs["L1_ffn_W_k"].T),
            "WvT": np.ascontiguousarray(inputs["L1_ffn_W_v"].T),
            "mur_rs": _rs(inputs["L1_ffn_mu_r"]),
            "muk_rs": _rs(inputs["L1_ffn_mu_k"])}


def run_l1_ffn(inputs, X3f):
    nc = build_l1_ffn()
    common = prep_l1_ffn(inputs)
    in_maps = []
    for i in range(NCORES):
        m = dict(common)
        m["x3"] = np.ascontiguousarray(X3f[B_PER * i:B_PER * (i + 1)])
        in_maps.append(m)
    res = run_bass_kernel_spmd(nc, in_maps, core_ids=list(range(NCORES)))
    return np.concatenate([r["x4"] for r in res.results], axis=0)


def emit_ln_y1(nc, p, psX, x1_ap, stage_fn, coll_fn, nb=B_PER, pfx="y1"):
    with ExitStack() as ctx:
        lns = [make_ln(nc, ctx, p, SJ, ps_stat=psX[3 - c][0:64, 0:512],
                       ps_bc=psX[3 - c][:, 512:1024], pfx=pfx + "ab"[c],
                       use_s7=(c == 0)) for c in range(2)]
        ybuf = [sbt(nc, ctx, f"y1buf{c}", [128, 2 * SJ], BF16)
                for c in range(2)]
        prevs = [None, None]
        war = {}
        for b in range(nb):
            for j in range(NJ):
                c = j % 2
                def xrows(dc, b=b, j=j):
                    return x1_ap[b, dc * 128:(dc + 1) * 128,
                                 j * SJ:(j + 1) * SJ]
                def yv(dc, c=c):
                    return ybuf[c][:, (dc % 2) * SJ:(dc % 2 + 1) * SJ]
                def ywar(dc, c=c):
                    return [war.get((c, dc % 2, 0)), war.get((c, dc % 2, 1))]
                def ydone(dc, ty, b=b, j=j, c=c):
                    sl = dc % 2
                    for r in (0, 1):
                        tok = p.dma(
                            "sync", stage_fn(b, j, dc, r),
                            ybuf[c][:, sl * SJ + r * POS:
                                    sl * SJ + (r + 1) * POS],
                            sem=pfx + "s%d%d%d" % (c, sl, r), waits=[ty])
                        war[(c, sl, r)] = tok
                prevs[c] = lns[c](xrows, yv, prev=prevs[c],
                                  y_done=ydone, y_war=ywar)
            w = [(pfx + "s%d%d%d" % (c, sl, r),
                  p.counts.get(pfx + "s%d%d%d" % (c, sl, r), 0))
                 for c in (0, 1) for sl in (0, 1) for r in (0, 1)]
            coll_fn(b, [t for t in w if t[1]])
    return


def emit_x2(nc, p, x1_ap, a2a_h_out, x2_ap, nb=B_PER, pfx="x2"):
    with ExitStack() as ctx:
        hb = [sbt(nc, ctx, f"x2h{i}", [128, SJ], BF16) for i in range(2)]
        xb = [sbt(nc, ctx, f"x2x{i}", [128, SJ], F32) for i in range(2)]
        ob = [sbt(nc, ctx, f"x2o{i}", [128, SJ], F32) for i in range(2)]
        st = [None, None]
        ta_prev = [None, None]
        ld = [None, None, None, None]
        n = 0
        for b in range(nb):
            for ec in range(NDC):
                for j in range(NJ):
                    sl = n % 2
                    n += 1
                    th = []
                    for hh in range(2):
                        row0 = ((2 * j + hh) * nb + b) * D + ec * 128
                        tt = p.dma("gpsimd", hb[sl][:, hh * POS:(hh + 1) * POS],
                                   a2a_h_out.ap()[row0:row0 + 128, :],
                                   sem=pfx + "h%d%d" % (sl, hh),
                                   waits=[ta_prev[sl]])
                        th.append(tt)
                    tx = p.dma("sync", xb[sl][:],
                               x1_ap[b, ec * 128:(ec + 1) * 128,
                                     j * SJ:(j + 1) * SJ],
                               sem=pfx + "x%d" % sl, waits=[ta_prev[sl]])
                    ta = p.dve("tensor_add", ob[sl][:], xb[sl][:], hb[sl][:],
                               waits=th + [tx, st[sl]])
                    ta_prev[sl] = ta
                    st[sl] = p.dma("gpsimd",
                                   x2_ap[b, ec * 128:(ec + 1) * 128,
                                         j * SJ:(j + 1) * SJ],
                                   ob[sl][:], sem=pfx + "s%d" % sl, waits=[ta])
    return st


def build_fused(nb=B_PER):
    nc = bass.Bass("TRN2", target_bir_lowering=False)
    ein = lambda nm, shp: nc.dram_tensor(nm, shp, F32, kind="ExternalInput").ap()
    drA = {"x": ein("x", [nb, D, S]),
           "wqkvT": ein("wqkvT", [D, 3 * D]),
           "woutT": ein("woutT", [D, D]),
           "bout_rs": ein("bout_rs", [128, NDC]),
           "cos_rep": ein("cos_rep", [128, S]),
           "sin_rep": ein("sin_rep", [128, S])}
    wg_d = {nm: ein(d, [D, D]) for nm, d in
            (("wf", "wfT"), ("uf", "ufT"), ("wi", "wiT"), ("ui", "uiT"),
             ("wc", "wcT"), ("uc", "ucT"), ("wo", "woT"), ("uo", "uoT"))}
    bias_d = {nm: ein(d, [128, NDC]) for nm, d in
              (("bf", "bf_rs"), ("bi", "bi_rs"), ("bc", "bc_rs"),
               ("bo", "bo_rs"), ("inc", "inc_rs"))}
    drA1 = {"Wq": ein("Wq", [D, D]), "Wk": ein("Wk", [D, D]),
            "Wv": ein("Wv", [D, D]), "woutT": ein("Wo1", [D, D]),
            "bout_rs": ein("bo1_rs", [128, NDC])}
    drF1 = {"WrT": ein("WrT", [D, D]), "WkT": ein("WkT", [D, D]),
            "WvT": ein("WvT", [D, D]), "mur_rs": ein("mur_rs", [128, NDC]),
            "muk_rs": ein("muk_rs", [128, NDC])}
    x1_t = nc.dram_tensor("x1", [nb, D, S], F32, kind="Internal")
    x2_t = nc.dram_tensor("x2", [nb, D, S], F32, kind="Internal")
    x3_t = nc.dram_tensor("x3", [nb, D, S], F32, kind="Internal")
    a2a_y_in = [nc.dram_tensor("a2a_y_in%d" % k, [NCORES * D, POS], BF16,
                               kind="Internal") for k in range(nb)]
    a2a_y_out = [nc.dram_tensor("a2a_y_out%d" % k, [NCORES * D, POS], BF16,
                                kind="Internal") for k in range(nb)]
    a2a_h_in = [nc.dram_tensor("a2a_h_in%d" % k, [NCORES * D, POS], BF16,
                               kind="Internal") for k in range(B_PER)]
    a2a_h_out = [nc.dram_tensor("a2a_h_out%d" % k, [NCORES * D, POS], BF16,
                                kind="Internal") for k in range(B_PER)]
    x4_d = nc.dram_tensor("x4", [nb, D, S], F32, kind="ExternalOutput").ap()

    drA["x_out"] = x1_t.ap()
    drA1["x"] = x2_t.ap()
    drA1["x_out"] = x3_t.ap()
    drF1["x"] = x3_t.ap()
    drF1["x_out"] = x4_d

    with ExitStack() as octx:
        psX = [pst(nc, octx, f"ps{i}", [128, 1024]) for i in range(4)]
        p = Plan(nc, octx)

        def y_coll(k, waits):
            p.emit("gpsimd",
                   lambda e, k=k: e.collective_compute(
                       "AllToAll", AL.bypass,
                       replica_groups=[list(range(NCORES))],
                       ins=[a2a_y_in[k].ap().opt()],
                       outs=[a2a_y_out[k].ap().opt()]),
                   waits=waits, inc="ccy", inc_n=1)
        y_stage = lambda b, j, dc, r: \
            a2a_y_in[b].ap()[(2 * j + r) * D + dc * 128:
                             (2 * j + r) * D + dc * 128 + 128, :]
        emit_attn(nc, p, psX, drA, rope=True, nb=nb, pfx="a0")
        p.barrier()
        y_coll_args = []
        emit_ln_y1(nc, p, psX, x1_t.ap(), y_stage,
                   lambda k, w: y_coll_args.append((k, w)), nb)
        p.barrier(exclude=("ccy",))
        def ysrc(t):
            return a2a_y_out[t // NCORES].ap()[
                (t % NCORES) * D:(t % NCORES + 1) * D, :].rearrange(
                "(a b) c -> b a c", b=128)
        def hdst(t, mc):
            row = (t % NCORES) * D + mc * 128
            return a2a_h_in[t // NCORES].ap()[row:row + 128, :]

        def h_coll(k, waits):
            p.emit("gpsimd",
                   lambda e, k=k: e.collective_compute(
                       "AllToAll", AL.bypass,
                       replica_groups=[list(range(NCORES))],
                       ins=[a2a_h_in[k].ap().opt()],
                       outs=[a2a_h_out[k].ap().opt()]),
                   waits=waits, inc="cch", inc_n=1)
        x2sp = {"x1_ap": x1_t.ap(), "h_out": lambda k: a2a_h_out[k],
                "x2_ap": x2_t.ap(), "gate": lambda k: [("cch", k + 1)]}
        drF0 = {"wg": wg_d, "bias": bias_d, "ysrc": ysrc, "hdst": hdst,
                "collh": h_coll, "x2sp": x2sp,
                "pre_colls": lambda: [y_coll(k, w) for k, w in y_coll_args]}
        emit_ffn0_core(nc, p, psX, drF0, y_ready=[], pfx="f0",
                       ygate=lambda t: [("ccy", t // NCORES + 1)])
        p.barrier()
        emit_attn(nc, p, psX, drA1, rope=False, nb=nb, pfx="a1")
        p.barrier()
        emit_ffn1(nc, p, psX, drF1, nb=nb, pfx="f1")
        from concourse.aot_env import allow_ici_all_to_all
        with allow_ici_all_to_all():
            finish(nc, p)
    return nc


def prep_fused(inputs):
    m = {}
    m.update(prep_l0_attn(inputs))
    m.update(prep_l0_ffn(inputs))
    m.update(prep_l1_ffn(inputs))
    m["Wq"] = inputs["L1_attn_Wq"]
    m["Wk"] = inputs["L1_attn_Wk"]
    m["Wv"] = inputs["L1_attn_Wv"]
    m["Wo1"] = inputs["L1_attn_Wo"]
    m["bo1_rs"] = _rs(inputs["L1_attn_bo"])
    return m


def _perm_qk():
    perm = np.zeros(D, dtype=np.int64)
    for n in range(512):
        perm[n] = 64 * (n // 32) + 2 * (n % 32)
        perm[512 + n] = 64 * (n // 32) + 2 * (n % 32) + 1
    return perm


def _rope_tables_rep():
    inv = (np.float32(1.0) / (np.float32(10000.0) **
           (np.arange(0, HD, 2, dtype=np.float32) / np.float32(HD)))).astype(np.float32)
    t = np.arange(S, dtype=np.float32)
    fr = np.outer(t, inv).astype(np.float32)
    cos = np.cos(fr).astype(np.float32).T
    sin = np.sin(fr).astype(np.float32).T
    return (np.ascontiguousarray(np.tile(cos, (4, 1))),
            np.ascontiguousarray(np.tile(sin, (4, 1))))


def _rs(v):
    return np.ascontiguousarray(v.reshape(NDC, 128).T)


def prep_l0_attn(inputs):
    wT = np.ascontiguousarray(inputs["L0_attn_w_qkv"].T)
    perm = _perm_qk()
    wqkvT = np.ascontiguousarray(np.concatenate(
        [wT[:, 0:D][:, perm], wT[:, D:2 * D][:, perm], wT[:, 2 * D:3 * D]],
        axis=1))
    cos_rep, sin_rep = _rope_tables_rep()
    return {"wqkvT": wqkvT,
            "woutT": np.ascontiguousarray(inputs["L0_attn_w_out"].T),
            "bout_rs": _rs(inputs["L0_attn_b_out"]),
            "cos_rep": cos_rep, "sin_rep": sin_rep}


def prep_l1_attn(inputs):
    return {"Wq": inputs["L1_attn_Wq"], "Wk": inputs["L1_attn_Wk"],
            "Wv": inputs["L1_attn_Wv"], "woutT": inputs["L1_attn_Wo"],
            "bout_rs": _rs(inputs["L1_attn_bo"])}


def run_attn(inputs, Xf, rope):
    nc = build_attn_layer(rope=rope)
    common = prep_l0_attn(inputs) if rope else prep_l1_attn(inputs)
    in_maps = []
    for i in range(NCORES):
        m = dict(common)
        m["x"] = np.ascontiguousarray(Xf[B_PER * i:B_PER * (i + 1)])
        in_maps.append(m)
    res = run_bass_kernel_spmd(nc, in_maps, core_ids=list(range(NCORES)))
    return np.concatenate([r["x1"] for r in res.results], axis=0)


def kernel(**inputs):
    Xf = np.ascontiguousarray(inputs["L0_attn_x"].transpose(0, 2, 1))
    nc = build_fused()
    common = prep_fused(inputs)
    in_maps = []
    for i in range(NCORES):
        m = dict(common)
        m["x"] = np.ascontiguousarray(Xf[i::NCORES])
        in_maps.append(m)
    res = run_bass_kernel_spmd(nc, in_maps, core_ids=list(range(NCORES)))
    X4f = np.empty((NCORES * B_PER, D, S), np.float32)
    for i in range(NCORES):
        X4f[i::NCORES] = res.results[i]["x4"]
    return np.ascontiguousarray(X4f.transpose(0, 2, 1))


def kernel_unfused(**inputs):
    Xf = np.ascontiguousarray(inputs["L0_attn_x"].transpose(0, 2, 1))
    X1f = run_attn(inputs, Xf, rope=True)
    X2f = run_l0_ffn(inputs, X1f)
    X3f = run_attn(inputs, X2f, rope=False)
    X4f = run_l1_ffn(inputs, X3f)
    return np.ascontiguousarray(X4f.transpose(0, 2, 1))
